# version 02
# baseline (speedup 1.0000x reference)
import contextlib
import sys

import numpy as np

try:
    import concourse.bass as bass
except ImportError:
    sys.path.insert(0, "/opt/trn_rl_repo")
    import concourse.bass as bass
import concourse.mybir as mybir
from concourse.bass_isa import ReduceOp
from concourse.bass_utils import run_bass_kernel_spmd

F32 = mybir.dt.float32
BF16 = mybir.dt.bfloat16
F16 = mybir.dt.float16
AFT = mybir.ActivationFunctionType
ALU = mybir.AluOpType
AX = mybir.AxisListType

B, S, H = 4, 2048, 2048
NH, NKV, HD = 16, 4, 128
DH = 5632
TQ = 1024
NTC = 16
NQT = 8
NHC = 16
NHB = 22
NDB = 8
EPS_LN = 1.0e-5
EPS_Q = 1.0e-5
SCAL = float(HD) ** -0.5
EXP_BIAS = -4.0


def _split3(x):
    c1 = np.float32(np.ldexp(np.floor(np.ldexp(x, 14)), -14))
    r = x - float(c1)
    c2 = np.float32(np.ldexp(np.floor(np.ldexp(r, 28)), -28))
    c3 = np.float32(r - float(c2))
    return float(c1), float(c2), float(c3)


TWO_PI = 2.0 * np.pi
C2PI_1, C2PI_2, C2PI_3 = _split3(TWO_PI)
PI32 = float(np.float32(3.14159247))
INV2PI = float(np.float32(1.0 / TWO_PI))
PIHALF = float(np.float32(np.pi / 2.0))
TPI_HI = float(np.float32(TWO_PI))
TPI_LO = float(np.float32(TWO_PI - float(np.float32(TWO_PI))))
SC_HI = float(np.float32(SCAL))
SC_LO = float(np.float32(SCAL - float(np.float32(SCAL))))

ARENA = 212288
O_TB, O_FR, O_KK, O_U0, O_B0 = 0, 8192, 16384, 24576, 32768
O_XTH = 0
O_WQT0, O_WQT1 = 32768, 49152
O_WKT, O_WVT = 65536, 81920
O_KT, O_VV = 98304, 114688
O_QT = 131072
O_AOUT = 0
O_WOT = 65536
O_XRES0 = 32768
O_Y1T0 = 49152
O_XINTT = 163840
O_XQT = 131072
O_HH = 0
O_WUP = 98304
O_T1 = 98304
O_T2 = 0
O_W2P0, O_W2P1 = 45056, 67584
O_W2S0, O_W2S1 = 143360, 154624
O_TBL = 163840
O_Y1N = 167936
O_MST0, O_MST1 = 186368, 190464
O_USCR, O_BSCR, O_TMP1 = 200704, 203520, 206336
O_KRAW, O_KROT = 200704, 201728
O_TMPK, O_TMPK2 = 202752, 204800
O_RHI = 206848
O_ST4 = 203520
O_WST = 90112
O_TSQ = 90112
O_OUTT0, O_OUTT1 = 0, 8192
O_COLS = 209152
O_ONESB = 209536
O_ONESR = 209600
O_P64 = 209888
O_PT0, O_PT1 = 210176, 211200

C_M1, C_M2, C_M3 = 0, 1, 2
C_TH1P, C_TH1N, C_TH2P, C_TH2N, C_TH3P, C_TH3N = 3, 4, 5, 6, 7, 8
C_IVH, C_IVL = 9, 10
C_SMAX, C_C1, C_C3, C_CH, C_C2 = 16, 24, 32, 40, 48
C_LS, C_LQ = 56, 64
C_T0, C_T1c, C_T2c, C_T3c, C_T4c = 72, 73, 74, 75, 76
C_LM, C_LV, C_LR, C_LR2 = 86, 87, 88, 89
C_MACC = 80
C_EB = 84


class Sched:
    ENGS = ("sync", "gpsimd", "vector", "scalar", "tensor")

    def __init__(self):
        self.ops = {e: [] for e in self.ENGS}
        self.val = {}
        self.cuts = []

    def sem(self, name):
        self.val.setdefault(name, 0)

    def wait(self, eng, name, val=None):
        v = self.val[name] if val is None else val
        if v > 0:
            self.ops[eng].append(("w", name, v))

    def do(self, eng, fn, inc=None, amt=1):
        if inc is not None:
            self.val[inc] += amt
            self.ops[eng].append(("i", fn, inc, amt))
        else:
            self.ops[eng].append(("i", fn, None, 0))

    def cutpoint(self):
        self.cuts.append({e: len(self.ops[e]) for e in self.ENGS})

    def barrier(self):
        self.cutpoint()
        marks = {}
        for e in self.ENGS:
            s = "bar_" + e
            self.val[s] += 1
            self.ops[e].append(("i", (lambda ee: ee.nop()), s, 1))
            marks[e] = self.val[s]
        for e in self.ENGS:
            for o in self.ENGS:
                if o != e:
                    self.ops[e].append(("w", "bar_" + o, marks[o]))


def build_program(cut=None):
    nc = bass.Bass("TRN2", target_bir_lowering=False, debug=False,
                   detect_race_conditions=False)

    def din(name, shape, dt=F32):
        return nc.dram_tensor(name, shape, dt, kind="ExternalInput").ap()

    xT = din("xT", [H, S])
    xres = din("xres", [TQ, H])
    tbc = din("tbc", [128, S])
    ivh = din("invf_hi", [128, 1])
    ivl = din("invf_lo", [128, 1])
    wqT = din("wqT", [H, H])
    wkT = din("wkT", [H, 512])
    wvT = din("wvT", [H, 512])
    woT = din("woT", [H, H])
    w1T = din("w1T", [H, DH])
    w3T = din("w3T", [H, DH])
    w2T = din("w2T", [DH, H])
    p64 = din("p64", [128, 128])
    ones32 = din("ones32", [128, 32])
    onesr = din("onesr", [1, 128])
    out = nc.dram_tensor("out", [TQ, H], F32, kind="ExternalOutput").ap()

    es = contextlib.ExitStack()
    arena = es.enter_context(nc.sbuf_tensor("arena", [128, ARENA], mybir.dt.uint8))
    base = nc.lookup_mloc(arena).addr

    def vt(name, shape, dtype, off):
        h = nc.virtual_sbuf_tensor(name, shape, dtype)
        nc._tensor(h.name, h.shape, h.dtype, type="SB")
        mloc = nc.lookup_mloc(h)
        mloc.allocated = True
        mloc.addr = base + off
        return h

    TB = vt("TB", [128, 2048], F32, O_TB)
    FR = vt("FR", [128, 2048], F32, O_FR)
    KKt = vt("KKt", [128, 2048], F32, O_KK)
    U0 = vt("U0", [128, 2048], F32, O_U0)
    B0 = vt("B0", [128, 2048], F32, O_B0)
    XTH = vt("XTH", [128, 16384], BF16, O_XTH)
    WQT = [vt("WQT0", [128, 8192], BF16, O_WQT0), vt("WQT1", [128, 8192], BF16, O_WQT1)]
    WKT = vt("WKT", [128, 8192], BF16, O_WKT)
    WVT = vt("WVT", [128, 8192], BF16, O_WVT)
    KT = vt("KT", [128, 8192], BF16, O_KT)
    VV = vt("VV", [128, 8192], BF16, O_VV)
    QT = vt("QT", [128, 16384], BF16, O_QT)
    AOUT = vt("AOUT", [128, 16384], BF16, O_AOUT)
    WOT = vt("WOT", [128, 32768], BF16, O_WOT)
    XRS = [vt("XRS0", [128, 2048], F32, O_XRES0), vt("XRS1", [128, 2048], F32, O_XRES0 + 8192)]
    Y1T = [vt("Y1Ta", [128, 2048], F32, O_Y1T0), vt("Y1Tb", [128, 2048], F32, O_Y1T0 + 8192)]
    XINTT = vt("XINTT", [128, 2048], BF16, O_XINTT)
    XQT = vt("XQT", [128, 16384], BF16, O_XQT)
    HHt = vt("HHt", [128, 45056], F16, O_HH)
    WUP = vt("WUP", [128, 16384], BF16, O_WUP)
    T1 = vt("T1", [128, 22528], F16, O_T1)
    T2 = vt("T2", [128, 22528], F16, O_T2)
    W2P = [vt("W2P0", [128, 11264], F16, O_W2P0), vt("W2P1", [128, 11264], F16, O_W2P1)]
    W2S = [vt("W2S0", [128, 2816], F32, O_W2S0), vt("W2S1", [128, 2816], F32, O_W2S1)]
    TBL = vt("TBLS", [128, 8192], F16, O_TBL)
    Y1N = [vt(f"Y1N{t}", [128, 2048], BF16, O_Y1N + 4096 * t) for t in range(8)]
    MST = [vt("MST0", [128, 1024], F32, O_MST0), vt("MST1", [128, 1024], F32, O_MST1)]
    USCR = vt("USCR", [128, 704], F32, O_USCR)
    BSCR = vt("BSCR", [128, 704], F32, O_BSCR)
    TMP1 = vt("TMP1", [128, 704], F32, O_TMP1)
    KRAW = vt("KRAW", [128, 512], BF16, O_KRAW)
    KROT = vt("KROT", [128, 512], BF16, O_KROT)
    TMPK = vt("TMPK", [128, 512], F32, O_TMPK)
    TMPK2 = vt("TMPK2", [128, 512], F32, O_TMPK2)
    RF = vt("RF", [1, 512], F32, O_TMPK)
    RHI = vt("RHI", [1, 512], BF16, O_RHI)
    RLO = vt("RLO", [1, 512], BF16, O_RHI + 1024)
    ST4 = [vt(f"ST4_{i}", [128, 256], F32, O_ST4 + 1024 * i) for i in range(4)]
    WST = [vt(f"WST{i}", [128, 512], F32, O_WST + 2048 * i) for i in range(4)]
    TSQ = vt("TSQ", [128, 2048], F32, O_TSQ)
    OUTT = [vt("OUTT0", [128, 2048], F32, O_OUTT0), vt("OUTT1", [128, 2048], F32, O_OUTT1)]
    COLS = vt("COLS", [128, 96], F32, O_COLS)
    ONESB = vt("ONESB", [128, 32], BF16, O_ONESB)
    ONESR = vt("ONESR", [1, 128], BF16, O_ONESR)
    P64S = vt("P64S", [128, 128], BF16, O_P64)
    PT = [vt("PTa", [128, 512], BF16, O_PT0), vt("PTb", [128, 512], BF16, O_PT1)]

    pb = [es.enter_context(nc.psum_tensor(f"pb{i}", [128, 512], F32)) for i in range(8)]

    sems = {}
    sem_names = (["c0", "m0", "m1", "x0", "x1", "tX", "tH", "u0", "u1", "u2", "u3",
                  "s0", "s1", "g0", "g1", "gq0", "gq1", "gw1", "gw2", "gc",
                  "a0", "a1", "eT", "eV", "eC", "eG"] + ["bar_" + e for e in Sched.ENGS])
    for s in sem_names:
        sems[s] = es.enter_context(nc.semaphore(s))

    sch = Sched()
    for s in sem_names:
        sch.sem(s)

    def C(i):
        return COLS[:, i:i + 1]

    def vsync():
        sch.wait("vector", "eV", sch.val["eV"])

    def vsmall(fn):
        sch.do("vector", fn, "eV")
        vsync()

    sch.do("sync", lambda e: e.dma_start(TB[:, :], tbc), "c0", 16)
    sch.do("sync", lambda e: e.dma_start(COLS[:, C_IVH:C_IVH + 1], ivh), "c0", 16)
    sch.do("sync", lambda e: e.dma_start(COLS[:, C_IVL:C_IVL + 1], ivl), "c0", 16)
    sch.do("gpsimd", lambda e: e.dma_start(ONESB[:, :], ones32), "gc", 16)
    sch.do("gpsimd", lambda e: e.dma_start(ONESR[:, :], onesr), "gc", 16)
    sch.do("gpsimd", lambda e: e.dma_start(P64S[:, :], p64), "gc", 16)
    mk_cl = sch.val["c0"]

    sch.cutpoint()
    sch.wait("vector", "c0", mk_cl)
    sch.do("vector", lambda e: e.tensor_scalar(FR[:, :], TB[:, :], C(C_IVH), None, ALU.mult))
    sch.do("vector", lambda e: e.scalar_tensor_tensor(FR[:, :], TB[:, :], C(C_IVL), FR[:, :], ALU.mult, ALU.add))

    def rope_reduce(src_extra):
        if src_extra != 0.0:
            sch.do("vector", (lambda c: lambda e: e.tensor_scalar(KKt[:, :], FR[:, :], c, None, ALU.add))(src_extra))
            fr_ap = KKt
        else:
            fr_ap = FR
        sch.do("vector", (lambda f: lambda e: e.tensor_scalar(B0[:, :], f[:, :], INV2PI, None, ALU.mult))(fr_ap))
        first = True
        for j in range(9, -1, -1):
            k = float(2 ** j)
            thr = k - 512.5
            src = B0 if first else U0
            sch.do("vector", (lambda s, kk, th: lambda e: e.tensor_scalar(TB[:, :], s[:, :], th, -kk, ALU.is_ge, ALU.mult))(src, k, thr))
            sch.do("vector", (lambda s: lambda e: e.tensor_tensor(U0[:, :], s[:, :], TB[:, :], ALU.add))(src))
            first = False
        sch.do("vector", lambda e: e.tensor_tensor(TB[:, :], B0[:, :], U0[:, :], ALU.subtract))
        sch.do("vector", lambda e: e.tensor_scalar(TB[:, :], TB[:, :], -512.0, None, ALU.add))
        sch.do("vector", (lambda f, c: lambda e: e.scalar_tensor_tensor(KKt[:, :], TB[:, :], -c, f[:, :], ALU.mult, ALU.add))(fr_ap, C2PI_1))
        for c in (C2PI_2, C2PI_3):
            sch.do("vector", (lambda cc: lambda e: e.scalar_tensor_tensor(KKt[:, :], TB[:, :], -cc, KKt[:, :], ALU.mult, ALU.add))(c))
        sch.do("vector", lambda e: e.tensor_scalar(KKt[:, :], KKt[:, :], -PI32, PI32, ALU.max, ALU.min), "eV")
        return sch.val["eV"]

    def table_out(dst_off_f16, dst_off_scaled):
        sch.do("vector", (lambda o: lambda e: e.tensor_copy(TBL[:, o:o + 2048], U0[:, :]))(dst_off_f16))
        sch.do("vector", lambda e: e.tensor_scalar(B0[:, :], U0[:, :], SC_HI, None, ALU.mult))
        sch.do("vector", lambda e: e.scalar_tensor_tensor(B0[:, :], U0[:, :], SC_LO, B0[:, :], ALU.mult, ALU.add))
        sch.do("vector", (lambda o: lambda e: e.tensor_copy(TBL[:, o:o + 2048], B0[:, :]))(dst_off_scaled), "eV")

    mk_r = rope_reduce(0.0)
    sch.wait("scalar", "eV", mk_r)
    sch.do("scalar", lambda e: e.activation(U0[:, :], KKt[:, :], AFT.Sin), "eC")
    sch.wait("vector", "eC", sch.val["eC"])
    table_out(0, 4096)
    mk_r2 = rope_reduce(PIHALF)
    sch.wait("scalar", "eV", mk_r2)
    sch.do("scalar", lambda e: e.activation(U0[:, :], KKt[:, :], AFT.Sin), "eC")
    sch.wait("vector", "eC", sch.val["eC"])
    table_out(2048, 6144)
    mk_tbl = sch.val["eV"]
    sch.cutpoint()

    sch.do("vector", lambda e: e.memset(COLS[:, C_MACC:C_MACC + 3], 0.0))
    sch.do("vector", lambda e: e.memset(COLS[:, C_EB:C_EB + 1], EXP_BIAS), "eV")
    mean_marks = {}
    mean_red = {}
    mean_jobs = []
    for mat_ap, rows, idx, w in [(w1T, H, 0, 704), (w3T, H, 1, 704), (w2T, DH, 2, 1024)]:
        cols = mat_ap.shape[1]
        assert cols % w == 0
        for rc in range(rows // 128):
            for cc in range(cols // w):
                mean_jobs.append((mat_ap[rc * 128:(rc + 1) * 128, cc * w:(cc + 1) * w], idx, w))
    mean_state = {"next": 0, "glob": 0, "idx": {}}

    def pump_mean(k):
        for _ in range(k):
            if mean_state["next"] >= len(mean_jobs):
                return
            src, idx, w = mean_jobs[mean_state["next"]]
            mean_state["next"] += 1
            g = mean_state["glob"]
            par = g % 2
            if g >= 2:
                sch.wait("sync", "eV", mean_marks[g - 2])
            ms = "m%d" % par
            sch.do("sync", (lambda d, s_: lambda e: e.dma_start(d, s_))(MST[par][:, 0:w], src), ms, 16)
            mk = sch.val[ms]
            sch.wait("vector", ms, mk)
            sch.do("vector", (lambda p, w_: lambda e: e.reduce_sum(C(C_T0 + p), MST[p][:, 0:w_], AX.X, apply_absolute_value=True))(par, w), "eV")
            mean_red[g] = sch.val["eV"]
            if g >= 1:
                pidx = mean_state["idx"][g - 1]
                sch.wait("vector", "eV", mean_red[g - 1])
                sch.do("vector", (lambda p, a: lambda e: e.tensor_tensor(C(a), C(a), C(C_T0 + p), ALU.add))((g - 1) % 2, C_MACC + pidx), "eV")
            else:
                sch.do("vector", lambda e: e.engine_nop(), "eV")
            mean_marks[g] = sch.val["eV"]
            mean_state["idx"][g] = idx
            mean_state["glob"] += 1

    sch.do("gpsimd", lambda e: e.dma_start(WKT[:, :].rearrange("p (hc o) -> p hc o", hc=16), wkT.rearrange("(hc p) o -> p hc o", p=128)), "g0", 16)
    sch.do("gpsimd", lambda e: e.dma_start(WVT[:, :].rearrange("p (hc o) -> p hc o", hc=16), wvT.rearrange("(hc p) o -> p hc o", p=128)), "g0", 16)
    mk_wkv = sch.val["g0"]
    sch.wait("gpsimd", "eV", mk_tbl)
    sch.do("gpsimd", lambda e: e.dma_start(XTH[:, :].rearrange("p (hc t) -> p hc t", hc=16), xT[:, 0:1024].rearrange("(hc p) t -> p hc t", p=128)), "g1", 16)
    mk_xth0 = sch.val["g1"]
    sch.cutpoint()

    ev = {}

    def rope_evac(psum_bank, dst_ap, tok0, scaled):
        sch.do("vector", (lambda b: lambda e: e.tensor_copy(KRAW[:, :], pb[b][:, :]))(psum_bank), "eV")
        ev[("pb", psum_bank)] = sch.val["eV"]
        mk_raw = sch.val["eV"]
        sch.wait("tensor", "eV", mk_raw)
        sch.wait("tensor", "eV", ev.get("rotfree", 0))
        sch.do("tensor", lambda e: e.matmul(pb[2][:, :], P64S[:, :], KRAW[:, :], start=True, stop=True), "eT")
        mk_rot = sch.val["eT"]
        co = (6144 if scaled else 2048) + tok0
        so = (4096 if scaled else 0) + tok0
        sch.wait("vector", "eT", mk_rot)
        sch.do("vector", (lambda o: lambda e: e.tensor_tensor(TMPK[:, :], KRAW[:, :], TBL[:, o:o + 512], ALU.mult))(co))
        sch.do("vector", (lambda o: lambda e: e.tensor_tensor(TMPK2[:, :], pb[2][:, :], TBL[:, o:o + 512], ALU.mult))(so))
        sch.do("vector", (lambda d: lambda e: e.tensor_tensor(d, TMPK[:, :], TMPK2[:, :], ALU.add))(dst_ap), "eV")
        ev["rotfree"] = sch.val["eV"]
        return sch.val["eV"]

    def kv_proj_half(half, xmark):
        for g in range(NKV):
            for ts in range(2):
                pump_mean(2)
                bank = (g * 2 + ts) % 2
                tok0 = half * 1024 + ts * 512
                sch.wait("tensor", "eV", ev.get(("pb", bank), 0))
                sch.wait("tensor", "g0", mk_wkv)
                sch.wait("tensor", "g1", xmark)
                for hc in range(NHC):
                    sch.do("tensor", (lambda g_, hc_, ts_, b_: lambda e: e.matmul(
                        pb[b_][:, :], WKT[:, hc_ * 512 + g_ * 128:hc_ * 512 + g_ * 128 + 128],
                        XTH[:, hc_ * 1024 + ts_ * 512:hc_ * 1024 + ts_ * 512 + 512],
                        start=(hc_ == 0), stop=(hc_ == NHC - 1)))(g, hc, ts, bank),
                        "eT" if hc == NHC - 1 else None)
                sch.wait("vector", "eT", sch.val["eT"])
                rope_evac(bank, KT[:, g * 2048 + tok0:g * 2048 + tok0 + 512], tok0, True)
        for tcr in range(8):
            pump_mean(2)
            tc = half * 8 + tcr
            bank = tcr % 2
            sch.wait("tensor", "eV", ev.get(("pb", bank), 0))
            for hc in range(NHC):
                sch.do("tensor", (lambda tcr_, hc_, b_: lambda e: e.matmul(
                    pb[b_][:, :], XTH[:, hc_ * 1024 + tcr_ * 128:hc_ * 1024 + tcr_ * 128 + 128],
                    WVT[:, hc_ * 512:hc_ * 512 + 512],
                    start=(hc_ == 0), stop=(hc_ == NHC - 1)))(tcr, hc, bank),
                    "eT" if hc == NHC - 1 else None)
            sch.wait("vector", "eT", sch.val["eT"])
            sch.do("vector", (lambda tc_, b_: lambda e: e.tensor_copy(VV[:, tc_ * 512:tc_ * 512 + 512], pb[b_][:, :]))(tc, bank), "eV")
            ev[("pb", bank)] = sch.val["eV"]

    kv_proj_half(0, mk_xth0)
    sch.cutpoint()

    wq_done = {}
    for ob in range(4):
        sch.cutpoint()
        par = ob % 2
        if ob == 0:
            sch.wait("gpsimd", "eV", mk_tbl)
        if ob >= 2:
            sch.wait("gpsimd", "eT", wq_done.get(ob - 2, 0))
        sch.do("gpsimd", (lambda o: lambda e: e.dma_start(
            WQT[o % 2][:, :].rearrange("p (hc o) -> p hc o", hc=16), wqT[:, o * 512:(o + 1) * 512].rearrange("(hc p) o -> p hc o", p=128)))(ob), "gq%d" % (ob % 2), 16)
        mk_wq = sch.val["gq%d" % (ob % 2)]
        for oc2 in range(4):
            n = ob * 4 + oc2
            for ts in range(2):
                pump_mean(2)
                bank = (oc2 * 2 + ts) % 2
                sch.wait("tensor", "eV", ev.get(("pb", bank), 0))
                sch.wait("tensor", "gq%d" % (ob % 2), mk_wq)
                for hc in range(NHC):
                    sch.do("tensor", (lambda p_, oc_, hc_, ts_, b_: lambda e: e.matmul(
                        pb[b_][:, :], WQT[p_][:, hc_ * 512 + oc_ * 128:hc_ * 512 + oc_ * 128 + 128],
                        XTH[:, hc_ * 1024 + ts_ * 512:hc_ * 1024 + ts_ * 512 + 512],
                        start=(hc_ == 0), stop=(hc_ == NHC - 1)))(par, oc2, hc, ts, bank),
                        "eT" if hc == NHC - 1 else None)
                if oc2 == 3 and ts == 1:
                    wq_done[ob] = sch.val["eT"]
                sch.wait("vector", "eT", sch.val["eT"])
                rope_evac(bank, QT[:, n * 1024 + ts * 512:n * 1024 + ts * 512 + 512], ts * 512, False)

    sch.cutpoint()
    sch.wait("gpsimd", "eT", sch.val["eT"])
    sch.do("gpsimd", lambda e: e.dma_start(XTH[:, :].rearrange("p (hc t) -> p hc t", hc=16), xT[:, 1024:2048].rearrange("(hc p) t -> p hc t", p=128)), "g1", 16)
    kv_proj_half(1, sch.val["g1"])

    sch.barrier()

    sch.do("gpsimd", lambda e: e.dma_start(
        WOT[:, 0:16384].rearrange("p (hc o) -> p hc o", hc=16), woT[:, 0:1024].rearrange("(hc p) o -> p hc o", p=128)), "gw1", 16)

    pt_exp = {}
    pt_pv = {}
    pv_free = 0
    rb_free = 0
    for n in range(NH):
        g = n // 4
        for qh in range(2):
            pump_mean(6)
            sch.wait("tensor", "eV", pv_free)
            for kc in range(NTC):
                sbank = kc % 2
                ptb = kc % 2
                sch.wait("tensor", "eC", pt_exp.get(sbank, 0))
                sch.do("tensor", (lambda g_, kc_, n_, qh_, b_: lambda e: e.matmul(
                    pb[b_][:, :], KT[:, g_ * 2048 + kc_ * 128:g_ * 2048 + kc_ * 128 + 128],
                    QT[:, n_ * 1024 + qh_ * 512:n_ * 1024 + qh_ * 512 + 512],
                    start=True, stop=True))(g, kc, n, qh, sbank), "eT")
                mk_s = sch.val["eT"]
                sch.wait("scalar", "eT", mk_s)
                sch.wait("scalar", "eT", pt_pv.get(ptb, 0))
                sch.do("scalar", (lambda b_, p_: lambda e: e.activation(
                    PT[p_][:, :], pb[b_][:, :], AFT.Exp, bias=C(C_EB), scale=1.0))(sbank, ptb), "eC")
                pt_exp[sbank] = sch.val["eC"]
                mk_exp = sch.val["eC"]
                sch.wait("tensor", "eC", mk_exp)
                sch.do("tensor", (lambda kc_, g_, p_: lambda e: e.matmul(
                    pb[2][:, :], VV[:, kc_ * 512 + g_ * 128:kc_ * 512 + g_ * 128 + 128],
                    PT[p_][:, :], start=(kc_ == 0), stop=(kc_ == NTC - 1)))(kc, g, ptb),
                    None)
                sch.do("tensor", (lambda kc_, p_: lambda e: e.matmul(
                    pb[3][0:1, :], ONESB[:, 0:1], PT[p_][:, :],
                    start=(kc_ == 0), stop=(kc_ == NTC - 1)))(kc, ptb),
                    "eT")
                pt_pv[ptb] = sch.val["eT"]
            mk_pv = sch.val["eT"]
            sch.wait("vector", "eT", mk_pv)
            sch.do("vector", lambda e: e.reciprocal(RF[:, :], pb[3][0:1, :]))
            sch.do("vector", lambda e: e.tensor_copy(RHI[:, :], RF[:, :]))
            sch.do("vector", lambda e: e.tensor_tensor(RLO[:, :], RF[:, :], RHI[:, :], ALU.subtract))
            sch.do("vector", lambda e: e.tensor_copy(TMPK2[:, :], pb[2][:, :]), "eV")
            mk_rl = sch.val["eV"]
            pv_free = mk_rl
            sch.wait("tensor", "eV", mk_rl)
            sch.wait("tensor", "eV", rb_free)
            sch.do("tensor", lambda e: e.matmul(pb[4][:, :], ONESR[:, :], RHI[:, :], start=True, stop=False))
            sch.do("tensor", lambda e: e.matmul(pb[4][:, :], ONESR[:, :], RLO[:, :], start=False, stop=True), "eT")
            mk_rb = sch.val["eT"]
            sch.wait("vector", "eT", mk_rb)
            sch.do("vector", (lambda n_, qh_: lambda e: e.tensor_tensor(
                AOUT[:, n_ * 1024 + qh_ * 512:n_ * 1024 + qh_ * 512 + 512],
                TMPK2[:, :], pb[4][:, :], ALU.mult))(n, qh), "eV")
            rb_free = sch.val["eV"]
    mk_attn = sch.val["eV"]

    sch.wait("gpsimd", "eV", mk_attn)
    sch.do("gpsimd", lambda e: e.dma_start(
        WOT[:, 16384:32768].rearrange("p (hc o) -> p hc o", hc=16), woT[:, 1024:2048].rearrange("(hc p) o -> p hc o", p=128)), "gw2", 16)
    mk_wot = sch.val["gw2"]

    pump_mean(len(mean_jobs))
    g_last = mean_state["glob"] - 1
    sch.wait("vector", "eV", mean_red[g_last])
    vsmall((lambda p, a: lambda e: e.tensor_tensor(C(a), C(a), C(C_T0 + p), ALU.add))(g_last % 2, C_MACC + mean_state["idx"][g_last]))
    NEL = float(H * DH)
    vsmall(lambda e: e.memset(KROT[:, 2:3], 0.0))
    vsmall(lambda e: e.memset(KROT[0:1, 2:3], 1.0))
    for i in range(3):
        vsmall((lambda s: lambda e: e.tensor_copy(KRAW[:, 0:1], C(C_MACC + s)))(i))
        sch.wait("tensor", "eV", sch.val["eV"])
        sch.do("tensor", lambda e: e.matmul(pb[5][0:1, 1:2], KROT[:, 2:3], KRAW[:, 0:1], start=True, stop=True), "eT")
        sch.wait("vector", "eT", sch.val["eT"])
        vsmall(lambda e: e.tensor_copy(KROT[0:1, 3:4], pb[5][0:1, 1:2]))
        sch.wait("tensor", "eV", sch.val["eV"])
        sch.do("tensor", lambda e: e.matmul(pb[6][:, 1:2], ONESR[:, :], KROT[0:1, 3:4], start=True, stop=True), "eT")
        sch.wait("vector", "eT", sch.val["eT"])
        vsmall(lambda e: e.tensor_copy(KRAW[:, 2:3], pb[6][:, 1:2]))
        vsmall((lambda s: lambda e: e.tensor_tensor(TMPK2[:, 0:1], C(C_MACC + s), KRAW[:, 2:3], ALU.subtract))(i))
        vsmall(lambda e: e.tensor_copy(KROT[:, 0:1], TMPK2[:, 0:1]))
        vsmall(lambda e: e.tensor_tensor(TMPK[:, 0:1], TMPK2[:, 0:1], KROT[:, 0:1], ALU.subtract))
        vsmall(lambda e: e.tensor_copy(KRAW[:, 1:2], TMPK[:, 0:1]))
        sch.wait("tensor", "eV", sch.val["eV"])
        sch.do("tensor", lambda e: e.matmul(pb[5][0:1, 0:1], ONESB[:, 0:1], KRAW[:, 2:3], start=True, stop=False))
        sch.do("tensor", lambda e: e.matmul(pb[5][0:1, 0:1], ONESB[:, 0:1], KROT[:, 0:1], start=False, stop=False))
        sch.do("tensor", lambda e: e.matmul(pb[5][0:1, 0:1], ONESB[:, 0:1], KRAW[:, 1:2], start=False, stop=True), "eT")
        sch.wait("vector", "eT", sch.val["eT"])
        vsmall(lambda e: e.tensor_scalar(RF[0:1, 0:1], pb[5][0:1, 0:1], 1.0 / NEL, EPS_Q, ALU.mult, ALU.max))
        vsmall(lambda e: e.tensor_copy(RHI[0:1, 0:1], RF[0:1, 0:1]))
        vsmall(lambda e: e.tensor_tensor(TMPK2[0:1, 1:2], RF[0:1, 0:1], RHI[0:1, 0:1], ALU.subtract))
        vsmall(lambda e: e.tensor_copy(RLO[0:1, 0:1], TMPK2[0:1, 1:2]))
        vsmall(lambda e: e.tensor_tensor(TMPK[0:1, 1:2], TMPK2[0:1, 1:2], RLO[0:1, 0:1], ALU.subtract))
        vsmall(lambda e: e.tensor_copy(RLO[0:1, 1:2], TMPK[0:1, 1:2]))
        sch.wait("tensor", "eV", sch.val["eV"])
        sch.do("tensor", lambda e: e.matmul(pb[6][:, 0:1], ONESR[:, :], RHI[0:1, 0:1], start=True, stop=False))
        sch.do("tensor", lambda e: e.matmul(pb[6][:, 0:1], ONESR[:, :], RLO[0:1, 0:1], start=False, stop=False))
        sch.do("tensor", lambda e: e.matmul(pb[6][:, 0:1], ONESR[:, :], RLO[0:1, 1:2], start=False, stop=True), "eT")
        sch.wait("vector", "eT", sch.val["eT"])
        vsmall((lambda s: lambda e: e.tensor_copy(C(C_M1 + s), pb[6][:, 0:1]))(i))
    for i in range(3):
        vsmall((lambda s: lambda e: e.tensor_scalar(C(C_TH1P + 2 * s), C(C_M1 + s), 0.5, None, ALU.mult))(i))
        vsmall((lambda s: lambda e: e.tensor_scalar(C(C_TH1N + 2 * s), C(C_M1 + s), -0.5, None, ALU.mult))(i))
    mk_th = sch.val["eV"]

    sch.barrier()

    xqt_mk = {}
    xint_mk = {}
    sq_mk = {}
    for t in range(NQT):
        sch.cutpoint()
        par = t % 2
        if t >= 2:
            sch.wait("sync", "eV", xint_mk.get(t - 2, 0))
        xs_ = "x%d" % (t % 2)
        sch.do("sync", (lambda t_: lambda e: e.dma_start(XRS[t_ % 2][:, :], xres[t_ * 128:(t_ + 1) * 128, :]))(t), xs_, 16)
        mk_xr = sch.val[xs_]
        for ob in range(4):
            bank = ob % 2
            sch.wait("tensor", "eV", ev.get(("yb", bank), 0))
            sch.wait("tensor", "gw1", 16)
            if ob >= 2:
                sch.wait("tensor", "gw2", mk_wot)
            for hc in range(NHC):
                wo_off = (ob // 2) * 16384 + (ob % 2) * 512
                sch.do("tensor", (lambda t_, wo_, hc_, b_: lambda e: e.matmul(
                    pb[b_][:, :], AOUT[:, hc_ * 1024 + t_ * 128:hc_ * 1024 + t_ * 128 + 128],
                    WOT[:, wo_ + hc_ * 1024:wo_ + hc_ * 1024 + 512],
                    start=(hc_ == 0), stop=(hc_ == NHC - 1)))(t, wo_off, hc, bank),
                    "eT" if hc == NHC - 1 else None)
            mk_y = sch.val["eT"]
            sch.wait("vector", "eT", mk_y)
            sch.wait("vector", xs_, mk_xr)
            sch.do("vector", (lambda t_, ob_, b_: lambda e: e.tensor_tensor(
                Y1T[t_ % 2][:, ob_ * 512:ob_ * 512 + 512], pb[b_][:, :],
                XRS[t_ % 2][:, ob_ * 512:ob_ * 512 + 512], ALU.add))(t, ob, bank), "eV")
            ev[("yb", bank)] = sch.val["eV"]
        mk_y1 = sch.val["eV"]
        sch.cutpoint()
        yt = Y1T[par]
        xrt = XRS[par]
        sch.do("vector", (lambda y: lambda e: e.reduce_sum(C(C_LM), y[:, :], AX.X))(yt), "eV")
        vsync()
        sch.do("vector", (lambda y, d: lambda e: e.scalar_tensor_tensor(
            d[:, :], y[:, :], 1.0, y[:, :], ALU.mult, ALU.mult))(yt, xrt))
        sch.do("vector", (lambda d: lambda e: e.reduce_sum(C(C_LV), d[:, :], AX.X))(xrt), "eV")
        vsync()
        vsmall(lambda e: e.tensor_scalar(C(C_LM), C(C_LM), 1.0 / 2048.0, None, ALU.mult))
        sq_mk[t] = sch.val["eV"]
        vsmall(lambda e: e.tensor_scalar(C(C_LV), C(C_LV), 1.0 / 2048.0, EPS_LN, ALU.mult, ALU.add))
        vsmall(lambda e: e.tensor_tensor(C(C_LR), C(C_LM), C(C_LM), ALU.mult))
        vsmall(lambda e: e.tensor_tensor(C(C_LV), C(C_LV), C(C_LR), ALU.subtract))
        mk_var = sch.val["eV"]
        sch.cutpoint()
        sch.wait("scalar", "eV", mk_var)
        sch.do("scalar", lambda e: e.activation(C(C_LR), C(C_LV), AFT.Sqrt), "eC")
        sch.wait("vector", "eC", sch.val["eC"])
        vsmall(lambda e: e.reciprocal(C(C_LR2), C(C_LR)))
        for _ in range(2):
            vsmall(lambda e: e.tensor_tensor(C(C_LR), C(C_LR2), C(C_LR2), ALU.mult))
            vsmall(lambda e: e.tensor_tensor(C(C_LR), C(C_LR), C(C_LV), ALU.mult))
            vsmall(lambda e: e.tensor_scalar(C(C_LR), C(C_LR), -0.5, 1.5, ALU.mult, ALU.add))
            vsmall(lambda e: e.tensor_tensor(C(C_LR2), C(C_LR2), C(C_LR), ALU.mult))
        sch.do("vector", (lambda y, d: lambda e: e.tensor_scalar(
            d[:, :], y[:, :], C(C_LM), C(C_LR2), ALU.subtract, ALU.mult))(yt, xrt))
        sch.do("vector", (lambda t_, d: lambda e: e.tensor_copy(Y1N[t_][:, :], d[:, :]))(t, xrt), "eV")
        sch.cutpoint()
        sch.do("vector", (lambda d: lambda e: e.reduce_max(C(C_LV), d[:, :], AX.X, apply_absolute_value=True))(xrt), "eV")
        vsync()
        vsmall(lambda e: e.tensor_scalar(C(C_LV), C(C_LV), 1.0, EPS_Q, ALU.mult, ALU.max))
        vsmall((lambda t_: lambda e: e.tensor_copy(C(C_SMAX + t_), C(C_LV)))(t))
        vsmall(lambda e: e.reciprocal(C(C_LR), C(C_LV)))
        vsmall(lambda e: e.tensor_scalar(C(C_LR), C(C_LR), 127.0, None, ALU.mult))
        for sl in range(4):
            sch.do("vector", (lambda d, s_: lambda e: e.tensor_scalar(
                TMP1[:, 0:512], d[:, s_ * 512:s_ * 512 + 512], C(C_LR), None, ALU.mult))(xrt, sl))
            first = True
            for j in range(7, -1, -1):
                k = float(2 ** j)
                thr = k - 128.5
                src = TMP1 if first else USCR
                sch.do("vector", (lambda s, kk, th: lambda e: e.tensor_scalar(
                    BSCR[:, 0:512], s[:, 0:512], th, -kk, ALU.is_ge, ALU.mult))(src, k, thr))
                sch.do("vector", (lambda s: lambda e: e.tensor_tensor(
                    USCR[:, 0:512], s[:, 0:512], BSCR[:, 0:512], ALU.add))(src))
                first = False
            sch.do("vector", lambda e: e.tensor_tensor(BSCR[:, 0:512], TMP1[:, 0:512], USCR[:, 0:512], ALU.subtract))
            sch.do("vector", (lambda s_: lambda e: e.tensor_scalar(
                XINTT[:, s_ * 512:s_ * 512 + 512], BSCR[:, 0:512], -128.0, None, ALU.add))(sl), "eV")
        mk_xint = sch.val["eV"]
        xint_mk[t] = mk_xint
        sch.wait("vector", "eV", mk_th)
        vsmall((lambda t_: lambda e: e.tensor_scalar(C(C_LV), C(C_SMAX + t_), 1.0 / 127.0, None, ALU.mult))(t))
        vsmall((lambda t_: lambda e: e.tensor_tensor(C(C_C1 + t_), C(C_LV), C(C_M1), ALU.mult))(t))
        vsmall((lambda t_: lambda e: e.tensor_tensor(C(C_C3 + t_), C(C_LV), C(C_M2), ALU.mult))(t))
        sch.wait("sync", "eV", mk_xint)
        sch.do("sync", (lambda t_: lambda e: e.dma_start_transpose(
            XQT[:, :].rearrange("p (dc tt) -> p dc tt", dc=16)[:, :, t_ * 128:(t_ + 1) * 128],
            XINTT[:, :]))(t), "tX", 16)
        xqt_mk[t] = sch.val["tX"]

    sch.barrier()

    sch.do("vector", lambda e: e.memset(COLS[:, C_CH:C_CH + 8], 0.0), "eV")
    up_mm = {}
    tern_mk = {}
    for hb in range(NHB):
        par = hb % 2
        for mi, (mat, thp, thn, wofs) in enumerate(
                [(w1T, C_TH1P, C_TH1N, 0), (w3T, C_TH2P, C_TH2N, 8192)]):
            for pc in range(8):
                slot = (mi * 8 + pc) % 4
                st = WST[slot]
                key = ("st", slot)
                sch.wait("sync", "eV", tern_mk.get(key, 0))
                if hb >= 2:
                    sch.wait("sync", "eT", up_mm.get(hb - 2, 0))
                us_ = "u%d" % slot
                src = mat[:, hb * 256:(hb + 1) * 256].rearrange("(dc p) c -> p dc c", p=128)[:, 2 * pc:2 * pc + 2, :]
                sch.do("sync", (lambda d, s_: lambda e: e.dma_start(d[:, :].rearrange("p (dc c) -> p dc c", dc=2), s_))(st, src), us_, 16)
                mk_pc = sch.val[us_]
                sch.wait("vector", us_, mk_pc)
                dst0 = wofs + par * 4096 + pc * 512
                sch.do("vector", (lambda st_, tp: lambda e: e.tensor_scalar(
                    TMP1[:, 0:512], st_[:, :], C(tp), None, ALU.is_gt))(st, thp))
                sch.do("vector", (lambda st_, tn: lambda e: e.tensor_scalar(
                    USCR[:, 0:512], st_[:, :], C(tn), None, ALU.is_lt))(st, thn))
                sch.do("vector", (lambda d0: lambda e: e.tensor_tensor(
                    WUP[:, d0:d0 + 512], TMP1[:, 0:512], USCR[:, 0:512], ALU.subtract))(dst0), "eV")
                tern_mk[key] = sch.val["eV"]
        mk_tern = sch.val["eV"]
        for t in range(NQT):
            b1 = t % 2
            b3 = 2 + t % 2
            sch.wait("tensor", "eV", ev.get(("h1", b1), 0))
            sch.wait("tensor", "eV", ev.get(("h3", b3), 0))
            sch.wait("tensor", "eV", mk_tern)
            sch.wait("tensor", "tX", 128)
            for dc in range(NHC):
                sch.do("tensor", (lambda t_, dc_, p_, b_: lambda e: e.matmul(
                    pb[b_][:, 0:256], XQT[:, dc_ * 1024 + t_ * 128:dc_ * 1024 + t_ * 128 + 128],
                    WUP[:, p_ * 4096 + dc_ * 256:p_ * 4096 + dc_ * 256 + 256],
                    start=(dc_ == 0), stop=(dc_ == NHC - 1)))(t, dc, par, b1), None)
            for dc in range(NHC):
                sch.do("tensor", (lambda t_, dc_, p_, b_: lambda e: e.matmul(
                    pb[b_][:, 0:256], XQT[:, dc_ * 1024 + t_ * 128:dc_ * 1024 + t_ * 128 + 128],
                    WUP[:, 8192 + p_ * 4096 + dc_ * 256:8192 + p_ * 4096 + dc_ * 256 + 256],
                    start=(dc_ == 0), stop=(dc_ == NHC - 1)))(t, dc, par, b3),
                    "eT" if dc == NHC - 1 else None)
            mk_mm = sch.val["eT"]
            up_mm[hb] = mk_mm
            sch.wait("vector", "eT", mk_mm)
            sch.do("vector", (lambda b_, t_: lambda e: e.tensor_scalar(
                ST4[0][:, :], pb[b_][:, 0:256], C(C_C1 + t_), None, ALU.mult))(b1, t), "eV")
            mk_t1 = sch.val["eV"]
            ev[("h1", b1)] = mk_t1
            sch.wait("scalar", "eV", mk_t1)
            sch.do("scalar", lambda e: e.activation(ST4[1][:, :], ST4[0][:, :], AFT.Silu), "eC")
            mk_silu = sch.val["eC"]
            sch.wait("vector", "eC", mk_silu)
            ho = t * 5632 + hb * 256
            sch.do("vector", (lambda b_, t_, o: lambda e: e.scalar_tensor_tensor(
                HHt[:, o:o + 256], pb[b_][:, 0:256], C(C_C3 + t_), ST4[1][:, :], ALU.mult, ALU.mult))(b3, t, ho), "eV")
            ev[("h3", b3)] = sch.val["eV"]
            sch.do("vector", (lambda o: lambda e: e.reduce_max(
                C(C_T0), HHt[:, o:o + 256], AX.X, apply_absolute_value=True))(ho), "eV")
            vsync()
            sch.do("vector", (lambda t_: lambda e: e.tensor_tensor(C(C_CH + t_), C(C_CH + t_), C(C_T0), ALU.max))(t), "eV")

    sch.barrier()

    tr_mk = {}
    for t in range(NQT):
        vsmall((lambda t_: lambda e: e.tensor_scalar(C(C_T0), C(C_CH + t_), 1.0, EPS_Q, ALU.mult, ALU.max))(t))
        vsmall(lambda e: e.reciprocal(C(C_T1c), C(C_T0)))
        vsmall(lambda e: e.tensor_scalar(C(C_T1c), C(C_T1c), 127.0, None, ALU.mult))
        vsmall(lambda e: e.tensor_scalar(C(C_T2c), C(C_T0), 1.0 / 127.0, None, ALU.mult))
        vsmall((lambda t_: lambda e: e.tensor_tensor(C(C_C2 + t_), C(C_T2c), C(C_M3), ALU.mult))(t))
        for sl in range(8):
            o0 = t * 5632 + sl * 704
            sch.do("vector", (lambda o: lambda e: e.tensor_scalar(
                TMP1[:, :], HHt[:, o:o + 704], C(C_T1c), None, ALU.mult))(o0))
            first = True
            for j in range(7, -1, -1):
                k = float(2 ** j)
                thr = k - 128.5
                src = TMP1 if first else USCR
                sch.do("vector", (lambda s, kk, th: lambda e: e.tensor_scalar(
                    BSCR[:, :], s[:, :], th, -kk, ALU.is_ge, ALU.mult))(src, k, thr))
                sch.do("vector", (lambda s: lambda e: e.tensor_tensor(USCR[:, :], s[:, :], BSCR[:, :], ALU.add))(src))
                first = False
            sch.do("vector", lambda e: e.tensor_tensor(BSCR[:, :], TMP1[:, :], USCR[:, :], ALU.subtract))
            sch.do("vector", (lambda o: lambda e: e.tensor_scalar(
                HHt[:, o:o + 704], BSCR[:, :], -128.0, None, ALU.add))(o0), "eV")
        mk_lad = sch.val["eV"]
        sch.wait("sync", "eV", mk_lad)
        if t < 4:
            dst = T1[:, :].rearrange("p (hc tt) -> p hc tt", hc=44)[:, :, t * 128:(t + 1) * 128]
        else:
            dst = T2[:, :].rearrange("p (hc tt) -> p hc tt", hc=44)[:, :, (t - 4) * 128:(t - 3) * 128]
            sch.wait("sync", "tH", tr_mk[3])
        sch.do("sync", (lambda t_, d: lambda e: e.dma_start_transpose(d, HHt[:, t_ * 5632:(t_ + 1) * 5632]))(t, dst), "tH", 16)
        tr_mk[t] = sch.val["tH"]
    mk_trdone = sch.val["tH"]

    sch.barrier()

    sch.do("vector", lambda e: e.memset(COLS[:, C_LS:C_LS + 16], 0.0), "eV")
    sch.wait("vector", "tH", mk_trdone)
    w2t_mk = {}
    zev_mk = {}
    zmm_mk = {}
    for db in range(NDB):
        par = db % 2
        if db >= 2:
            sch.wait("sync", "eT", zmm_mk[db - 2])
        for sla in range(4):
            stg = W2S[sla % 2]
            sch.wait("sync", "eV", w2t_mk.get((db - 1, sla + 2), 0) if db >= 1 else 0)
            if sla >= 2:
                sch.wait("sync", "eV", w2t_mk.get((db, sla - 2), 0))
            ss_ = "s%d" % (sla % 2)
            src = w2T[:, db * 256:(db + 1) * 256].rearrange("(hc p) c -> p hc c", p=128)[:, sla * 11:(sla + 1) * 11, :]
            sch.do("sync", (lambda d, s_: lambda e: e.dma_start(d[:, :].rearrange("p (hc c) -> p hc c", hc=11), s_))(stg, src), ss_, 16)
            mk_sl = sch.val[ss_]
            sch.wait("vector", ss_, mk_sl)
            for sl in range(4):
                o0 = sl * 704
                sch.do("vector", (lambda st_, o: lambda e: e.tensor_scalar(
                    TMP1[:, :], st_[:, o:o + 704], C(C_TH3P), None, ALU.is_gt))(stg, o0))
                sch.do("vector", (lambda st_, o: lambda e: e.tensor_scalar(
                    USCR[:, :], st_[:, o:o + 704], C(C_TH3N), None, ALU.is_lt))(stg, o0))
                sch.do("vector", (lambda p_, sa, o: lambda e: e.tensor_tensor(
                    W2P[p_][:, sa * 2816 + o:sa * 2816 + o + 704], TMP1[:, :], USCR[:, :], ALU.subtract))(par, sla, o0), "eV")
            w2t_mk[(db, sla)] = sch.val["eV"]
        mk_w2t = sch.val["eV"]
        for t in range(NQT):
            bank = par * 4 + (t % 4)
            half = (t // 4) * 256
            sch.wait("tensor", "eV", zev_mk.get((db - 2, t), 0))
            sch.wait("tensor", "eV", mk_w2t)
            if db == 0:
                sch.wait("tensor", "tH", mk_trdone)
            src_t = T1 if t < 4 else T2
            for hc in range(44):
                sch.do("tensor", (lambda s_, hc_, t_, b_, h_, p_: lambda e: e.matmul(
                    pb[b_][:, h_:h_ + 256],
                    s_[:, hc_ * 512 + (t_ % 4) * 128:hc_ * 512 + (t_ % 4) * 128 + 128],
                    W2P[p_][:, hc_ * 256:hc_ * 256 + 256],
                    start=(hc_ == 0), stop=(hc_ == 43)))(src_t, hc, t, bank, half, par),
                    "eT" if hc == 43 else None)
        mk_zmm = sch.val["eT"]
        zmm_mk[db] = mk_zmm
        for t in range(NQT):
            bank = par * 4 + (t % 4)
            half = (t // 4) * 256
            sch.wait("vector", "eT", mk_zmm)
            zsl = (lambda t_, d_: Y1N[t_][:, d_ * 256:d_ * 256 + 256])(t, db)
            sch.do("vector", (lambda b_, h_, t_, z: lambda e: e.scalar_tensor_tensor(
                z, pb[b_][:, h_:h_ + 256], C(C_C2 + t_), z, ALU.mult, ALU.add))(bank, half, t, zsl), "eV")
            zev_mk[(db, t)] = sch.val["eV"]
            mk_z = sch.val["eV"]
            sch.do("vector", (lambda z: lambda e: e.reduce_sum(C(C_T1c), z, AX.X))(zsl), "eV")
            vsync()
            sch.do("vector", (lambda t_: lambda e: e.tensor_tensor(C(C_LS + t_), C(C_LS + t_), C(C_T1c), ALU.add))(t))
            sch.do("vector", (lambda z: lambda e: e.scalar_tensor_tensor(
                TSQ[:, 0:256], z, 1.0, z, ALU.mult, ALU.mult))(zsl))
            sch.do("vector", lambda e: e.reduce_sum(C(C_T2c), TSQ[:, 0:256], AX.X), "eV")
            vsync()
            sch.do("vector", (lambda t_: lambda e: e.tensor_tensor(C(C_LQ + t_), C(C_LQ + t_), C(C_T2c), ALU.add))(t), "eV")

    sch.barrier()

    st_mk = {}
    for t in range(NQT):
        par = t % 2
        if t >= 2:
            sch.wait("vector", "a%d" % (t % 2), st_mk[t - 2])
        vsmall((lambda t_: lambda e: e.tensor_scalar(C(C_T0), C(C_LS + t_), 1.0 / 2048.0, None, ALU.mult))(t))
        vsmall((lambda t_: lambda e: e.tensor_scalar(C(C_T1c), C(C_LQ + t_), 1.0 / 2048.0, EPS_LN, ALU.mult, ALU.add))(t))
        vsmall(lambda e: e.tensor_tensor(C(C_T2c), C(C_T0), C(C_T0), ALU.mult))
        vsmall(lambda e: e.tensor_tensor(C(C_T1c), C(C_T1c), C(C_T2c), ALU.subtract))
        sch.wait("scalar", "eV", sch.val["eV"])
        sch.do("scalar", lambda e: e.activation(C(C_T2c), C(C_T1c), AFT.Sqrt), "eC")
        sch.wait("vector", "eC", sch.val["eC"])
        vsmall(lambda e: e.reciprocal(C(C_T3c), C(C_T2c)))
        for _ in range(2):
            vsmall(lambda e: e.tensor_tensor(C(C_T2c), C(C_T3c), C(C_T3c), ALU.mult))
            vsmall(lambda e: e.tensor_tensor(C(C_T2c), C(C_T2c), C(C_T1c), ALU.mult))
            vsmall(lambda e: e.tensor_scalar(C(C_T2c), C(C_T2c), -0.5, 1.5, ALU.mult, ALU.add))
            vsmall(lambda e: e.tensor_tensor(C(C_T3c), C(C_T3c), C(C_T2c), ALU.mult))
        sch.do("vector", (lambda t_, p_: lambda e: e.tensor_scalar(
            OUTT[p_][:, :], Y1N[t_][:, :], C(C_T0), C(C_T3c), ALU.subtract, ALU.mult))(t, par), "eV")
        mk_o = sch.val["eV"]
        sch.wait("scalar", "eV", mk_o)
        sch.do("scalar", (lambda t_, p_: lambda e: e.dma_start(out[t_ * 128:(t_ + 1) * 128, :], OUTT[p_][:, :]))(t, par), "a%d" % par, 16)
        st_mk[t] = sch.val["a%d" % par]
    sch.wait("scalar", "a0", sch.val["a0"])
    sch.wait("scalar", "a1", sch.val["a1"])
    sch.wait("sync", "a0", sch.val["a0"])
    sch.wait("sync", "a1", sch.val["a1"])

    if cut is not None:
        sch.ops = {e: sch.ops[e][:sch.cuts[cut][e]] for e in sch.ops}
        nv = {s: 0 for s in sch.val}
        for e in sch.ops:
            for op in sch.ops[e]:
                if op[0] == "i" and op[2] is not None:
                    nv[op[2]] += op[3]
        sch.val = nv
    import os
    dumps = os.environ.get("KDUMP", "")
    if dumps:
        sch.barrier()
        loc = locals()
        for i, nm in enumerate(dumps.split(",")):
            if ":" in nm:
                b, ix = nm.split(":")
                th = loc[b][int(ix)]
                nm = b + ix
            else:
                th = loc[nm]
            dsh = list(th.shape)
            dout = nc.dram_tensor(f"dump_{nm}", dsh, th.dtype, kind="ExternalOutput").ap()
            sch.do("sync", (lambda d, t: lambda e: e.dma_start(d, t[:, :]))(dout, th), "c0", 16)
        sch.wait("sync", "c0", sch.val["c0"])
    _QUEUE = {"c0": "sync", "m0": "sync", "m1": "sync", "x0": "sync", "x1": "sync",
              "tX": "sync", "tH": "sync", "u0": "sync", "u1": "sync", "u2": "sync",
              "u3": "sync", "s0": "sync", "s1": "sync",
              "g0": "gpsimd", "g1": "gpsimd", "gq0": "gpsimd", "gq1": "gpsimd",
              "gw1": "gpsimd", "gw2": "gpsimd", "gc": "gpsimd",
              "a0": "scalar", "a1": "scalar"}
    totals = {s: 0 for s in _QUEUE}
    for e in sch.ops:
        for op in sch.ops[e]:
            if op[0] == "i" and op[2] in totals:
                totals[op[2]] += op[3]
    for s, eng in _QUEUE.items():
        if totals[s]:
            sch.ops[eng].append(("w", s, totals[s]))
    with es:
        with nc.Block() as block:
            def make(ename):
                def body(eng):
                    for op in sch.ops[ename]:
                        if op[0] == "w":
                            eng.wait_ge(sems[op[1]], op[2])
                        else:
                            ins = op[1](eng)
                            if op[2] is not None:
                                ins.then_inc(sems[op[2]], op[3])
                return body

            block.sync(make("sync"))
            block.gpsimd(make("gpsimd"))
            block.vector(make("vector"))
            block.scalar(make("scalar"))
            block.tensor(make("tensor"))
    global _LAST_SCHED, _LAST_TENSORS
    _LAST_SCHED = sch
    _LAST_TENSORS = {k: v for k, v in locals().items()
                     if k in ("QT", "KT", "VV", "AOUT", "XQT", "XINTT", "HHt",
                              "T1", "T2", "Y1N", "TBL", "COLS", "WUP", "W2P", "XTH", "WOT")}
    return nc


_LAST_SCHED = None


_NC_CACHE = None


def _host_inputs(L0_attn_x, L0_attn_Wq, L0_attn_Wk, L0_attn_Wv, L0_attn_Wo,
                 L0_ffn_w1, L0_ffn_w2, L0_ffn_w3):
    x = np.ascontiguousarray(np.asarray(L0_attn_x, dtype=np.float32))
    wq = np.asarray(L0_attn_Wq, dtype=np.float32)
    wk = np.asarray(L0_attn_Wk, dtype=np.float32)
    wv = np.asarray(L0_attn_Wv, dtype=np.float32)
    wo = np.asarray(L0_attn_Wo, dtype=np.float32)
    w1 = np.asarray(L0_ffn_w1, dtype=np.float32)
    w2 = np.asarray(L0_ffn_w2, dtype=np.float32)
    w3 = np.asarray(L0_ffn_w3, dtype=np.float32)

    shared = {
        "wqT": np.ascontiguousarray(wq.T),
        "wkT": np.ascontiguousarray(wk.T),
        "wvT": np.ascontiguousarray(wv.T),
        "woT": np.ascontiguousarray(wo.T),
        "w1T": np.ascontiguousarray(w1.T),
        "w3T": np.ascontiguousarray(w3.T),
        "w2T": np.ascontiguousarray(w2.T),
    }
    inv64 = 1.0 / (10000.0 ** (np.arange(0, HD, 2).astype(np.float64) / HD))
    invf = np.concatenate([inv64, inv64])
    ivh = invf.astype(np.float32)
    ivl = (invf - ivh.astype(np.float64)).astype(np.float32)
    shared["invf_hi"] = np.ascontiguousarray(ivh.reshape(128, 1))
    shared["invf_lo"] = np.ascontiguousarray(ivl.reshape(128, 1))
    p64 = np.zeros((128, 128), dtype=np.float32)
    for d in range(64):
        p64[d + 64, d] = -1.0
        p64[d, d + 64] = 1.0
    shared["p64"] = p64
    shared["ones32"] = np.ones((128, 32), dtype=np.float32)
    shared["onesr"] = np.ones((1, 128), dtype=np.float32)

    in_maps = []
    for c in range(8):
        b, half = c // 2, c % 2
        perm = np.concatenate([np.arange(half * TQ, (half + 1) * TQ),
                               np.arange((1 - half) * TQ, (2 - half) * TQ)])
        m = dict(shared)
        m["xT"] = np.ascontiguousarray(x[b].T[:, perm])
        m["xres"] = np.ascontiguousarray(x[b, half * TQ:(half + 1) * TQ, :])
        m["tbc"] = np.ascontiguousarray(
            np.broadcast_to(perm.astype(np.float32)[None, :], (128, S)))
        in_maps.append(m)
    return in_maps


def kernel(L0_attn_x, L0_attn_Wq, L0_attn_Wk, L0_attn_Wv, L0_attn_Wo,
           L0_ffn_w1, L0_ffn_w2, L0_ffn_w3):
    global _NC_CACHE
    in_maps = _host_inputs(L0_attn_x, L0_attn_Wq, L0_attn_Wk, L0_attn_Wv,
                           L0_attn_Wo, L0_ffn_w1, L0_ffn_w2, L0_ffn_w3)
    if _NC_CACHE is None:
        _NC_CACHE = build_program()
    res = run_bass_kernel_spmd(_NC_CACHE, in_maps, core_ids=list(range(8)))
    y = np.empty((B, S, H), dtype=np.float32)
    for c in range(8):
        b, half = c // 2, c % 2
        y[b, half * TQ:(half + 1) * TQ, :] = np.asarray(res.results[c]["out"])
    return y


# version 03
# speedup vs baseline: 1.0116x; 1.0116x over previous
import contextlib
import sys

import numpy as np

try:
    import concourse.bass as bass
except ImportError:
    sys.path.insert(0, "/opt/trn_rl_repo")
    import concourse.bass as bass
import concourse.mybir as mybir
from concourse.bass_isa import ReduceOp
from concourse.bass_utils import run_bass_kernel_spmd

F32 = mybir.dt.float32
BF16 = mybir.dt.bfloat16
F16 = mybir.dt.float16
AFT = mybir.ActivationFunctionType
ALU = mybir.AluOpType
AX = mybir.AxisListType

B, S, H = 4, 2048, 2048
NH, NKV, HD = 16, 4, 128
DH = 5632
TQ = 1024
NTC = 16
NQT = 8
NHC = 16
NHB = 22
NDB = 8
EPS_LN = 1.0e-5
EPS_Q = 1.0e-5
SCAL = float(HD) ** -0.5
EXP_BIAS = -4.0


def _split3(x):
    c1 = np.float32(np.ldexp(np.floor(np.ldexp(x, 14)), -14))
    r = x - float(c1)
    c2 = np.float32(np.ldexp(np.floor(np.ldexp(r, 28)), -28))
    c3 = np.float32(r - float(c2))
    return float(c1), float(c2), float(c3)


SC_HI = float(np.float32(SCAL))
SC_LO = float(np.float32(SCAL - float(np.float32(SCAL))))
NELI = 1.0 / float(H * DH)
NELI_HI = float(np.float32(NELI))
NELI_LO = float(np.float32(NELI - NELI_HI))
I127 = 1.0 / 127.0
I127_HI = float(np.float32(I127))
I127_LO = float(np.float32(I127 - I127_HI))

ARENA = 212288
O_TB, O_FR, O_KK, O_U0, O_B0 = 0, 8192, 16384, 24576, 32768
O_XTH = 0
O_WQT0, O_WQT1 = 32768, 49152
O_WKT, O_WVT = 65536, 81920
O_KT, O_VV = 98304, 114688
O_QT = 131072
O_AOUT = 0
O_WOT = 65536
O_XRES0 = 32768
O_Y1T0 = 49152
O_XINTT = 163840
O_XQT = 131072
O_HH = 0
O_WUP = 98304
O_T1 = 98304
O_T2 = 0
O_W2P0, O_W2P1 = 45056, 67584
O_W2S0, O_W2S1 = 143360, 154624
O_TBL = 163840
O_Y1N = 167936
O_MST0, O_MST1 = 186368, 190464
O_USCR, O_BSCR, O_TMP1 = 200704, 203520, 206336
O_KRAW, O_KROT = 200704, 201728
O_TMPK, O_TMPK2 = 202752, 204800
O_RHI = 206848
O_ST4 = 203520
O_WST = 90112
O_TSQ = 90112
O_OUTT0, O_OUTT1 = 0, 8192
O_COLS = 209152
O_ONESB = 209536
O_ONESR = 209600
O_P64 = 209888
O_PT0, O_PT1 = 210176, 211200

C_M1, C_M2, C_M3 = 0, 1, 2
C_TH1P, C_TH1N, C_TH2P, C_TH2N, C_TH3P, C_TH3N = 3, 4, 5, 6, 7, 8
C_IVH, C_IVL = 9, 10
C_SMAX, C_C1, C_C3, C_CH, C_C2 = 16, 24, 32, 40, 48
C_LS, C_LQ = 56, 64
C_T0, C_T1c, C_T2c, C_T3c, C_T4c = 72, 73, 74, 75, 76
C_LM, C_LV, C_LR, C_LR2 = 86, 87, 88, 89
C_MACC = 80
C_EB = 84


class Sched:
    ENGS = ("sync", "gpsimd", "vector", "scalar", "tensor")

    def __init__(self):
        self.ops = {e: [] for e in self.ENGS}
        self.val = {}
        self.cuts = []

    def sem(self, name):
        self.val.setdefault(name, 0)

    def wait(self, eng, name, val=None):
        v = self.val[name] if val is None else val
        if v > 0:
            self.ops[eng].append(("w", name, v))

    def do(self, eng, fn, inc=None, amt=1):
        if inc is not None:
            self.val[inc] += amt
            self.ops[eng].append(("i", fn, inc, amt))
        else:
            self.ops[eng].append(("i", fn, None, 0))

    def cutpoint(self):
        self.cuts.append({e: len(self.ops[e]) for e in self.ENGS})

    def barrier(self):
        self.cutpoint()
        marks = {}
        for e in self.ENGS:
            s = "bar_" + e
            self.val[s] += 1
            self.ops[e].append(("i", (lambda ee: ee.nop()), s, 1))
            marks[e] = self.val[s]
        for e in self.ENGS:
            for o in self.ENGS:
                if o != e:
                    self.ops[e].append(("w", "bar_" + o, marks[o]))


def build_program(cut=None):
    nc = bass.Bass("TRN2", target_bir_lowering=False, debug=False,
                   detect_race_conditions=False)

    def din(name, shape, dt=F32):
        return nc.dram_tensor(name, shape, dt, kind="ExternalInput").ap()

    xT = din("xT", [H, S])
    xres = din("xres", [TQ, H])
    rope = din("rope", [128, 4096])
    wqT = din("wqT", [H, H])
    wkT = din("wkT", [H, 512])
    wvT = din("wvT", [H, 512])
    woT = din("woT", [H, H])
    w1T = din("w1T", [H, DH])
    w3T = din("w3T", [H, DH])
    w2T = din("w2T", [DH, H])
    p64 = din("p64", [128, 128])
    ones32 = din("ones32", [128, 32])
    onesr = din("onesr", [1, 128])
    out = nc.dram_tensor("out", [TQ, H], F32, kind="ExternalOutput").ap()

    es = contextlib.ExitStack()
    arena = es.enter_context(nc.sbuf_tensor("arena", [128, ARENA], mybir.dt.uint8))
    base = nc.lookup_mloc(arena).addr

    def vt(name, shape, dtype, off):
        h = nc.virtual_sbuf_tensor(name, shape, dtype)
        nc._tensor(h.name, h.shape, h.dtype, type="SB")
        mloc = nc.lookup_mloc(h)
        mloc.allocated = True
        mloc.addr = base + off
        return h

    TB = vt("TB", [128, 2048], F32, O_TB)
    FR = vt("FR", [128, 2048], F32, O_FR)
    KKt = vt("KKt", [128, 2048], F32, O_KK)
    U0 = vt("U0", [128, 2048], F32, O_U0)
    B0 = vt("B0", [128, 2048], F32, O_B0)
    XTH = vt("XTH", [128, 16384], BF16, O_XTH)
    WQT = [vt("WQT0", [128, 8192], BF16, O_WQT0), vt("WQT1", [128, 8192], BF16, O_WQT1)]
    WKT = vt("WKT", [128, 8192], BF16, O_WKT)
    WVT = vt("WVT", [128, 8192], BF16, O_WVT)
    KT = vt("KT", [128, 8192], BF16, O_KT)
    VV = vt("VV", [128, 8192], BF16, O_VV)
    QT = vt("QT", [128, 16384], BF16, O_QT)
    AOUT = vt("AOUT", [128, 16384], BF16, O_AOUT)
    WOT = vt("WOT", [128, 32768], BF16, O_WOT)
    XRS = [vt("XRS0", [128, 2048], F32, O_XRES0), vt("XRS1", [128, 2048], F32, O_XRES0 + 8192)]
    Y1T = [vt("Y1Ta", [128, 2048], F32, O_Y1T0), vt("Y1Tb", [128, 2048], F32, O_Y1T0 + 8192)]
    XINTT = vt("XINTT", [128, 2048], BF16, O_XINTT)
    XQT = vt("XQT", [128, 16384], BF16, O_XQT)
    HHt = vt("HHt", [128, 45056], F16, O_HH)
    WUP = vt("WUP", [128, 16384], BF16, O_WUP)
    T1 = vt("T1", [128, 22528], F16, O_T1)
    T2 = vt("T2", [128, 22528], F16, O_T2)
    W2P = [vt("W2P0", [128, 11264], F16, O_W2P0), vt("W2P1", [128, 11264], F16, O_W2P1)]
    W2S = [vt("W2S0", [128, 2816], F32, O_W2S0), vt("W2S1", [128, 2816], F32, O_W2S1)]
    TBL = vt("TBLS", [128, 8192], F16, O_TBL)
    Y1N = [vt(f"Y1N{t}", [128, 2048], BF16, O_Y1N + 4096 * t) for t in range(8)]
    MST = [vt("MST0", [128, 1024], F32, O_MST0), vt("MST1", [128, 1024], F32, O_MST1)]
    USCR = vt("USCR", [128, 704], F32, O_USCR)
    BSCR = vt("BSCR", [128, 704], F32, O_BSCR)
    TMP1 = vt("TMP1", [128, 704], F32, O_TMP1)
    KRAW = vt("KRAW", [128, 512], BF16, O_KRAW)
    KROT = vt("KROT", [128, 512], BF16, O_KROT)
    TMPK = vt("TMPK", [128, 512], F32, O_TMPK)
    TMPK2 = vt("TMPK2", [128, 512], F32, O_TMPK2)
    RF = vt("RF", [1, 512], F32, O_TMPK)
    RHI = vt("RHI", [1, 512], BF16, O_RHI)
    RLO = vt("RLO", [1, 512], BF16, O_RHI + 1024)
    ST4 = [vt(f"ST4_{i}", [128, 256], F32, O_ST4 + 1024 * i) for i in range(4)]
    WST = [vt(f"WST{i}", [128, 512], F32, O_WST + 2048 * i) for i in range(4)]
    TSQ = vt("TSQ", [128, 2048], F32, O_TSQ)
    OUTT = [vt("OUTT0", [128, 2048], F32, O_OUTT0), vt("OUTT1", [128, 2048], F32, O_OUTT1)]
    COLS = vt("COLS", [128, 96], F32, O_COLS)
    ONESB = vt("ONESB", [128, 32], BF16, O_ONESB)
    ONESR = vt("ONESR", [1, 128], BF16, O_ONESR)
    P64S = vt("P64S", [128, 128], BF16, O_P64)
    PT = [vt("PTa", [128, 512], BF16, O_PT0), vt("PTb", [128, 512], BF16, O_PT1)]

    pb = [es.enter_context(nc.psum_tensor(f"pb{i}", [128, 512], F32)) for i in range(8)]

    sems = {}
    sem_names = (["c0", "m0", "m1", "x0", "x1", "tX", "tH", "u0", "u1", "u2", "u3",
                  "s0", "s1", "g0", "g1", "gq0", "gq1", "gw1", "gw2", "gc",
                  "a0", "a1", "eT", "eV", "eC", "eG"] + ["bar_" + e for e in Sched.ENGS])
    for s in sem_names:
        sems[s] = es.enter_context(nc.semaphore(s))

    sch = Sched()
    for s in sem_names:
        sch.sem(s)

    def C(i):
        return COLS[:, i:i + 1]

    def vsync():
        sch.wait("vector", "eV", sch.val["eV"])

    def vsmall(fn):
        sch.do("vector", fn, "eV")
        vsync()

    sch.do("sync", lambda e: e.dma_start(TB[:, :], rope[:, 0:2048]), "c0", 16)
    sch.do("sync", lambda e: e.dma_start(FR[:, :], rope[:, 2048:4096]), "c0", 16)
    sch.do("gpsimd", lambda e: e.dma_start(ONESB[:, :], ones32), "gc", 16)
    sch.do("gpsimd", lambda e: e.dma_start(ONESR[:, :], onesr), "gc", 16)
    sch.do("gpsimd", lambda e: e.dma_start(P64S[:, :], p64), "gc", 16)
    mk_cl = sch.val["c0"]

    sch.cutpoint()
    sch.wait("vector", "c0", mk_cl)
    sch.do("vector", lambda e: e.tensor_copy(TBL[:, 0:2048], TB[:, :]))
    sch.do("vector", lambda e: e.tensor_copy(TBL[:, 2048:4096], FR[:, :]))
    sch.do("vector", lambda e: e.tensor_scalar(U0[:, :], TB[:, :], SC_HI, None, ALU.mult))
    sch.do("vector", lambda e: e.scalar_tensor_tensor(U0[:, :], TB[:, :], SC_LO, U0[:, :], ALU.mult, ALU.add))
    sch.do("vector", lambda e: e.tensor_copy(TBL[:, 4096:6144], U0[:, :]))
    sch.do("vector", lambda e: e.tensor_scalar(U0[:, :], FR[:, :], SC_HI, None, ALU.mult))
    sch.do("vector", lambda e: e.scalar_tensor_tensor(U0[:, :], FR[:, :], SC_LO, U0[:, :], ALU.mult, ALU.add))
    sch.do("vector", lambda e: e.tensor_copy(TBL[:, 6144:8192], U0[:, :]), "eV")
    mk_tbl = sch.val["eV"]
    sch.cutpoint()

    sch.do("vector", lambda e: e.memset(COLS[:, C_MACC:C_MACC + 3], 0.0))
    sch.do("vector", lambda e: e.memset(COLS[:, C_EB:C_EB + 1], EXP_BIAS), "eV")
    mean_marks = {}
    mean_red = {}
    mean_jobs = []
    for mat_ap, rows, idx, w in [(w1T, H, 0, 704), (w3T, H, 1, 704), (w2T, DH, 2, 1024)]:
        cols = mat_ap.shape[1]
        assert cols % w == 0
        for rc in range(rows // 128):
            for cc in range(cols // w):
                mean_jobs.append((mat_ap[rc * 128:(rc + 1) * 128, cc * w:(cc + 1) * w], idx, w))
    mean_state = {"next": 0, "glob": 0, "idx": {}}

    def pump_mean(k):
        for _ in range(k):
            if mean_state["next"] >= len(mean_jobs):
                return
            src, idx, w = mean_jobs[mean_state["next"]]
            mean_state["next"] += 1
            g = mean_state["glob"]
            par = g % 2
            if g >= 2:
                sch.wait("sync", "eV", mean_marks[g - 2])
            ms = "m%d" % par
            sch.do("sync", (lambda d, s_: lambda e: e.dma_start(d, s_))(MST[par][:, 0:w], src), ms, 16)
            mk = sch.val[ms]
            sch.wait("vector", ms, mk)
            sch.do("vector", (lambda p, w_: lambda e: e.reduce_sum(C(C_T0 + p), MST[p][:, 0:w_], AX.X, apply_absolute_value=True))(par, w), "eV")
            mean_red[g] = sch.val["eV"]
            if g >= 1:
                pidx = mean_state["idx"][g - 1]
                sch.wait("vector", "eV", mean_red[g - 1])
                sch.do("vector", (lambda p, a: lambda e: e.tensor_tensor(C(a), C(a), C(C_T0 + p), ALU.add))((g - 1) % 2, C_MACC + pidx), "eV")
            else:
                sch.do("vector", lambda e: e.engine_nop(), "eV")
            mean_marks[g] = sch.val["eV"]
            mean_state["idx"][g] = idx
            mean_state["glob"] += 1

    sch.do("gpsimd", lambda e: e.dma_start(WKT[:, :].rearrange("p (hc o) -> p hc o", hc=16), wkT.rearrange("(hc p) o -> p hc o", p=128)), "g0", 16)
    sch.do("gpsimd", lambda e: e.dma_start(WVT[:, :].rearrange("p (hc o) -> p hc o", hc=16), wvT.rearrange("(hc p) o -> p hc o", p=128)), "g0", 16)
    mk_wkv = sch.val["g0"]
    sch.wait("gpsimd", "eV", mk_tbl)
    sch.do("gpsimd", lambda e: e.dma_start(XTH[:, :].rearrange("p (hc t) -> p hc t", hc=16), xT[:, 0:1024].rearrange("(hc p) t -> p hc t", p=128)), "g1", 16)
    mk_xth0 = sch.val["g1"]
    sch.cutpoint()

    ev = {}

    def rope_evac(psum_bank, dst_ap, tok0, scaled):
        sch.do("vector", (lambda b: lambda e: e.tensor_copy(KRAW[:, :], pb[b][:, :]))(psum_bank), "eV")
        ev[("pb", psum_bank)] = sch.val["eV"]
        mk_raw = sch.val["eV"]
        sch.wait("tensor", "eV", mk_raw)
        sch.wait("tensor", "eV", ev.get("rotfree", 0))
        sch.do("tensor", lambda e: e.matmul(pb[2][:, :], P64S[:, :], KRAW[:, :], start=True, stop=True), "eT")
        mk_rot = sch.val["eT"]
        co = (6144 if scaled else 2048) + tok0
        so = (4096 if scaled else 0) + tok0
        sch.wait("vector", "eT", mk_rot)
        sch.do("vector", (lambda o: lambda e: e.tensor_tensor(TMPK[:, :], KRAW[:, :], TBL[:, o:o + 512], ALU.mult))(co))
        sch.do("vector", (lambda o: lambda e: e.tensor_tensor(TMPK2[:, :], pb[2][:, :], TBL[:, o:o + 512], ALU.mult))(so))
        sch.do("vector", (lambda d: lambda e: e.tensor_tensor(d, TMPK[:, :], TMPK2[:, :], ALU.add))(dst_ap), "eV")
        ev["rotfree"] = sch.val["eV"]
        return sch.val["eV"]

    def kv_proj_half(half, xmark):
        for g in range(NKV):
            for ts in range(2):
                pump_mean(2)
                bank = (g * 2 + ts) % 2
                tok0 = half * 1024 + ts * 512
                sch.wait("tensor", "eV", ev.get(("pb", bank), 0))
                sch.wait("tensor", "g0", mk_wkv)
                sch.wait("tensor", "g1", xmark)
                for hc in range(NHC):
                    sch.do("tensor", (lambda g_, hc_, ts_, b_: lambda e: e.matmul(
                        pb[b_][:, :], WKT[:, hc_ * 512 + g_ * 128:hc_ * 512 + g_ * 128 + 128],
                        XTH[:, hc_ * 1024 + ts_ * 512:hc_ * 1024 + ts_ * 512 + 512],
                        start=(hc_ == 0), stop=(hc_ == NHC - 1)))(g, hc, ts, bank),
                        "eT" if hc == NHC - 1 else None)
                sch.wait("vector", "eT", sch.val["eT"])
                rope_evac(bank, KT[:, g * 2048 + tok0:g * 2048 + tok0 + 512], tok0, True)
        for tcr in range(8):
            pump_mean(2)
            tc = half * 8 + tcr
            bank = tcr % 2
            sch.wait("tensor", "eV", ev.get(("pb", bank), 0))
            for hc in range(NHC):
                sch.do("tensor", (lambda tcr_, hc_, b_: lambda e: e.matmul(
                    pb[b_][:, :], XTH[:, hc_ * 1024 + tcr_ * 128:hc_ * 1024 + tcr_ * 128 + 128],
                    WVT[:, hc_ * 512:hc_ * 512 + 512],
                    start=(hc_ == 0), stop=(hc_ == NHC - 1)))(tcr, hc, bank),
                    "eT" if hc == NHC - 1 else None)
            sch.wait("vector", "eT", sch.val["eT"])
            sch.do("vector", (lambda tc_, b_: lambda e: e.tensor_copy(VV[:, tc_ * 512:tc_ * 512 + 512], pb[b_][:, :]))(tc, bank), "eV")
            ev[("pb", bank)] = sch.val["eV"]

    kv_proj_half(0, mk_xth0)
    sch.cutpoint()

    wq_done = {}
    for ob in range(4):
        sch.cutpoint()
        par = ob % 2
        if ob == 0:
            sch.wait("gpsimd", "eV", mk_tbl)
        if ob >= 2:
            sch.wait("gpsimd", "eT", wq_done.get(ob - 2, 0))
        sch.do("gpsimd", (lambda o: lambda e: e.dma_start(
            WQT[o % 2][:, :].rearrange("p (hc o) -> p hc o", hc=16), wqT[:, o * 512:(o + 1) * 512].rearrange("(hc p) o -> p hc o", p=128)))(ob), "gq%d" % (ob % 2), 16)
        mk_wq = sch.val["gq%d" % (ob % 2)]
        for oc2 in range(4):
            n = ob * 4 + oc2
            for ts in range(2):
                pump_mean(2)
                bank = (oc2 * 2 + ts) % 2
                sch.wait("tensor", "eV", ev.get(("pb", bank), 0))
                sch.wait("tensor", "gq%d" % (ob % 2), mk_wq)
                for hc in range(NHC):
                    sch.do("tensor", (lambda p_, oc_, hc_, ts_, b_: lambda e: e.matmul(
                        pb[b_][:, :], WQT[p_][:, hc_ * 512 + oc_ * 128:hc_ * 512 + oc_ * 128 + 128],
                        XTH[:, hc_ * 1024 + ts_ * 512:hc_ * 1024 + ts_ * 512 + 512],
                        start=(hc_ == 0), stop=(hc_ == NHC - 1)))(par, oc2, hc, ts, bank),
                        "eT" if hc == NHC - 1 else None)
                if oc2 == 3 and ts == 1:
                    wq_done[ob] = sch.val["eT"]
                sch.wait("vector", "eT", sch.val["eT"])
                rope_evac(bank, QT[:, n * 1024 + ts * 512:n * 1024 + ts * 512 + 512], ts * 512, False)

    sch.cutpoint()
    sch.wait("gpsimd", "eT", sch.val["eT"])
    sch.do("gpsimd", lambda e: e.dma_start(XTH[:, :].rearrange("p (hc t) -> p hc t", hc=16), xT[:, 1024:2048].rearrange("(hc p) t -> p hc t", p=128)), "g1", 16)
    kv_proj_half(1, sch.val["g1"])

    sch.barrier()

    sch.do("gpsimd", lambda e: e.dma_start(
        WOT[:, 0:16384].rearrange("p (hc o) -> p hc o", hc=16), woT[:, 0:1024].rearrange("(hc p) o -> p hc o", p=128)), "gw1", 16)

    pt_exp = {}
    pt_pv = {}
    pv_free = 0
    rb_free = 0
    for n in range(NH):
        g = n // 4
        for qh in range(2):
            pump_mean(6)
            sch.wait("tensor", "eV", pv_free)
            for kc in range(NTC):
                sbank = kc % 2
                ptb = kc % 2
                sch.wait("tensor", "eC", pt_exp.get(sbank, 0))
                sch.do("tensor", (lambda g_, kc_, n_, qh_, b_: lambda e: e.matmul(
                    pb[b_][:, :], KT[:, g_ * 2048 + kc_ * 128:g_ * 2048 + kc_ * 128 + 128],
                    QT[:, n_ * 1024 + qh_ * 512:n_ * 1024 + qh_ * 512 + 512],
                    start=True, stop=True))(g, kc, n, qh, sbank), "eT")
                mk_s = sch.val["eT"]
                sch.wait("scalar", "eT", mk_s)
                sch.wait("scalar", "eT", pt_pv.get(ptb, 0))
                sch.do("scalar", (lambda b_, p_: lambda e: e.activation(
                    PT[p_][:, :], pb[b_][:, :], AFT.Exp, bias=C(C_EB), scale=1.0))(sbank, ptb), "eC")
                pt_exp[sbank] = sch.val["eC"]
                mk_exp = sch.val["eC"]
                sch.wait("tensor", "eC", mk_exp)
                sch.do("tensor", (lambda kc_, g_, p_: lambda e: e.matmul(
                    pb[2][:, :], VV[:, kc_ * 512 + g_ * 128:kc_ * 512 + g_ * 128 + 128],
                    PT[p_][:, :], start=(kc_ == 0), stop=(kc_ == NTC - 1)))(kc, g, ptb),
                    None)
                sch.do("tensor", (lambda kc_, p_: lambda e: e.matmul(
                    pb[3][0:1, :], ONESB[:, 0:1], PT[p_][:, :],
                    start=(kc_ == 0), stop=(kc_ == NTC - 1)))(kc, ptb),
                    "eT")
                pt_pv[ptb] = sch.val["eT"]
            mk_pv = sch.val["eT"]
            sch.wait("vector", "eT", mk_pv)
            sch.do("vector", lambda e: e.reciprocal(RF[:, :], pb[3][0:1, :]))
            sch.do("vector", lambda e: e.tensor_copy(RHI[:, :], RF[:, :]))
            sch.do("vector", lambda e: e.tensor_tensor(RLO[:, :], RF[:, :], RHI[:, :], ALU.subtract))
            sch.do("vector", lambda e: e.tensor_copy(TMPK2[:, :], pb[2][:, :]), "eV")
            mk_rl = sch.val["eV"]
            pv_free = mk_rl
            sch.wait("tensor", "eV", mk_rl)
            sch.wait("tensor", "eV", rb_free)
            sch.do("tensor", lambda e: e.matmul(pb[4][:, :], ONESR[:, :], RHI[:, :], start=True, stop=False))
            sch.do("tensor", lambda e: e.matmul(pb[4][:, :], ONESR[:, :], RLO[:, :], start=False, stop=True), "eT")
            mk_rb = sch.val["eT"]
            sch.wait("vector", "eT", mk_rb)
            sch.do("vector", (lambda n_, qh_: lambda e: e.tensor_tensor(
                AOUT[:, n_ * 1024 + qh_ * 512:n_ * 1024 + qh_ * 512 + 512],
                TMPK2[:, :], pb[4][:, :], ALU.mult))(n, qh), "eV")
            rb_free = sch.val["eV"]
    mk_attn = sch.val["eV"]

    sch.wait("gpsimd", "eV", mk_attn)
    sch.do("gpsimd", lambda e: e.dma_start(
        WOT[:, 16384:32768].rearrange("p (hc o) -> p hc o", hc=16), woT[:, 1024:2048].rearrange("(hc p) o -> p hc o", p=128)), "gw2", 16)
    mk_wot = sch.val["gw2"]

    pump_mean(len(mean_jobs))
    g_last = mean_state["glob"] - 1
    sch.wait("vector", "eV", mean_red[g_last])
    vsmall((lambda p, a: lambda e: e.tensor_tensor(C(a), C(a), C(C_T0 + p), ALU.add))(g_last % 2, C_MACC + mean_state["idx"][g_last]))
    NEL = float(H * DH)
    vsmall(lambda e: e.memset(KROT[:, 2:3], 0.0))
    vsmall(lambda e: e.memset(KROT[0:1, 2:3], 1.0))
    for i in range(3):
        vsmall((lambda s: lambda e: e.tensor_copy(KRAW[:, 0:1], C(C_MACC + s)))(i))
        sch.wait("tensor", "eV", sch.val["eV"])
        sch.do("tensor", lambda e: e.matmul(pb[5][0:1, 1:2], KROT[:, 2:3], KRAW[:, 0:1], start=True, stop=True), "eT")
        sch.wait("vector", "eT", sch.val["eT"])
        vsmall(lambda e: e.tensor_copy(KROT[0:1, 3:4], pb[5][0:1, 1:2]))
        sch.wait("tensor", "eV", sch.val["eV"])
        sch.do("tensor", lambda e: e.matmul(pb[6][:, 1:2], ONESR[:, :], KROT[0:1, 3:4], start=True, stop=True), "eT")
        sch.wait("vector", "eT", sch.val["eT"])
        vsmall(lambda e: e.tensor_copy(KRAW[:, 2:3], pb[6][:, 1:2]))
        vsmall((lambda s: lambda e: e.tensor_tensor(TMPK2[:, 0:1], C(C_MACC + s), KRAW[:, 2:3], ALU.subtract))(i))
        vsmall(lambda e: e.tensor_copy(KROT[:, 0:1], TMPK2[:, 0:1]))
        vsmall(lambda e: e.tensor_tensor(TMPK[:, 0:1], TMPK2[:, 0:1], KROT[:, 0:1], ALU.subtract))
        vsmall(lambda e: e.tensor_copy(KRAW[:, 1:2], TMPK[:, 0:1]))
        sch.wait("tensor", "eV", sch.val["eV"])
        sch.do("tensor", lambda e: e.matmul(pb[5][0:1, 0:1], ONESB[:, 0:1], KRAW[:, 2:3], start=True, stop=False))
        sch.do("tensor", lambda e: e.matmul(pb[5][0:1, 0:1], ONESB[:, 0:1], KROT[:, 0:1], start=False, stop=False))
        sch.do("tensor", lambda e: e.matmul(pb[5][0:1, 0:1], ONESB[:, 0:1], KRAW[:, 1:2], start=False, stop=True), "eT")
        sch.wait("vector", "eT", sch.val["eT"])
        vsmall(lambda e: e.tensor_scalar(TMPK2[0:1, 3:4], pb[5][0:1, 0:1], NELI_HI, None, ALU.mult))
        vsmall(lambda e: e.scalar_tensor_tensor(RF[0:1, 0:1], pb[5][0:1, 0:1], NELI_LO, TMPK2[0:1, 3:4], ALU.mult, ALU.add))
        vsmall(lambda e: e.tensor_scalar(RF[0:1, 0:1], RF[0:1, 0:1], 1.0, EPS_Q, ALU.mult, ALU.max))
        vsmall(lambda e: e.tensor_copy(RHI[0:1, 0:1], RF[0:1, 0:1]))
        vsmall(lambda e: e.tensor_tensor(TMPK2[0:1, 1:2], RF[0:1, 0:1], RHI[0:1, 0:1], ALU.subtract))
        vsmall(lambda e: e.tensor_copy(RLO[0:1, 0:1], TMPK2[0:1, 1:2]))
        vsmall(lambda e: e.tensor_tensor(TMPK[0:1, 1:2], TMPK2[0:1, 1:2], RLO[0:1, 0:1], ALU.subtract))
        vsmall(lambda e: e.tensor_copy(RLO[0:1, 1:2], TMPK[0:1, 1:2]))
        sch.wait("tensor", "eV", sch.val["eV"])
        sch.do("tensor", lambda e: e.matmul(pb[6][:, 0:1], ONESR[:, :], RHI[0:1, 0:1], start=True, stop=False))
        sch.do("tensor", lambda e: e.matmul(pb[6][:, 0:1], ONESR[:, :], RLO[0:1, 0:1], start=False, stop=False))
        sch.do("tensor", lambda e: e.matmul(pb[6][:, 0:1], ONESR[:, :], RLO[0:1, 1:2], start=False, stop=True), "eT")
        sch.wait("vector", "eT", sch.val["eT"])
        vsmall((lambda s: lambda e: e.tensor_copy(C(C_M1 + s), pb[6][:, 0:1]))(i))
    for i in range(3):
        vsmall((lambda s: lambda e: e.tensor_scalar(C(C_TH1P + 2 * s), C(C_M1 + s), 0.5, None, ALU.mult))(i))
        vsmall((lambda s: lambda e: e.tensor_scalar(C(C_TH1N + 2 * s), C(C_M1 + s), -0.5, None, ALU.mult))(i))
    mk_th = sch.val["eV"]

    sch.barrier()

    xqt_mk = {}
    xint_mk = {}
    sq_mk = {}
    for t in range(NQT):
        sch.cutpoint()
        par = t % 2
        if t >= 2:
            sch.wait("sync", "eV", xint_mk.get(t - 2, 0))
        xs_ = "x%d" % (t % 2)
        sch.do("sync", (lambda t_: lambda e: e.dma_start(XRS[t_ % 2][:, :], xres[t_ * 128:(t_ + 1) * 128, :]))(t), xs_, 16)
        mk_xr = sch.val[xs_]
        for ob in range(4):
            bank = ob % 2
            sch.wait("tensor", "eV", ev.get(("yb", bank), 0))
            sch.wait("tensor", "gw1", 16)
            if ob >= 2:
                sch.wait("tensor", "gw2", mk_wot)
            for hc in range(NHC):
                wo_off = (ob // 2) * 16384 + (ob % 2) * 512
                sch.do("tensor", (lambda t_, wo_, hc_, b_: lambda e: e.matmul(
                    pb[b_][:, :], AOUT[:, hc_ * 1024 + t_ * 128:hc_ * 1024 + t_ * 128 + 128],
                    WOT[:, wo_ + hc_ * 1024:wo_ + hc_ * 1024 + 512],
                    start=(hc_ == 0), stop=(hc_ == NHC - 1)))(t, wo_off, hc, bank),
                    "eT" if hc == NHC - 1 else None)
            mk_y = sch.val["eT"]
            sch.wait("vector", "eT", mk_y)
            sch.wait("vector", xs_, mk_xr)
            sch.do("vector", (lambda t_, ob_, b_: lambda e: e.tensor_tensor(
                Y1T[t_ % 2][:, ob_ * 512:ob_ * 512 + 512], pb[b_][:, :],
                XRS[t_ % 2][:, ob_ * 512:ob_ * 512 + 512], ALU.add))(t, ob, bank), "eV")
            ev[("yb", bank)] = sch.val["eV"]
        mk_y1 = sch.val["eV"]
        sch.cutpoint()
        yt = Y1T[par]
        xrt = XRS[par]
        sch.do("vector", (lambda y: lambda e: e.reduce_sum(C(C_LM), y[:, :], AX.X))(yt), "eV")
        vsync()
        sch.do("vector", (lambda y, d: lambda e: e.scalar_tensor_tensor(
            d[:, :], y[:, :], 1.0, y[:, :], ALU.mult, ALU.mult))(yt, xrt))
        sch.do("vector", (lambda d: lambda e: e.reduce_sum(C(C_LV), d[:, :], AX.X))(xrt), "eV")
        vsync()
        vsmall(lambda e: e.tensor_scalar(C(C_LM), C(C_LM), 1.0 / 2048.0, None, ALU.mult))
        sq_mk[t] = sch.val["eV"]
        vsmall(lambda e: e.tensor_scalar(C(C_LV), C(C_LV), 1.0 / 2048.0, EPS_LN, ALU.mult, ALU.add))
        vsmall(lambda e: e.tensor_tensor(C(C_LR), C(C_LM), C(C_LM), ALU.mult))
        vsmall(lambda e: e.tensor_tensor(C(C_LV), C(C_LV), C(C_LR), ALU.subtract))
        mk_var = sch.val["eV"]
        sch.cutpoint()
        sch.wait("scalar", "eV", mk_var)
        sch.do("scalar", lambda e: e.activation(C(C_LR), C(C_LV), AFT.Sqrt), "eC")
        sch.wait("vector", "eC", sch.val["eC"])
        vsmall(lambda e: e.reciprocal(C(C_LR2), C(C_LR)))
        for _ in range(2):
            vsmall(lambda e: e.tensor_tensor(C(C_LR), C(C_LR2), C(C_LR2), ALU.mult))
            vsmall(lambda e: e.tensor_tensor(C(C_LR), C(C_LR), C(C_LV), ALU.mult))
            vsmall(lambda e: e.tensor_scalar(C(C_LR), C(C_LR), -0.5, 1.5, ALU.mult, ALU.add))
            vsmall(lambda e: e.tensor_tensor(C(C_LR2), C(C_LR2), C(C_LR), ALU.mult))
        sch.do("vector", (lambda y, d: lambda e: e.tensor_scalar(
            d[:, :], y[:, :], C(C_LM), C(C_LR2), ALU.subtract, ALU.mult))(yt, xrt))
        sch.do("vector", (lambda t_, d: lambda e: e.tensor_copy(Y1N[t_][:, :], d[:, :]))(t, xrt), "eV")
        sch.cutpoint()
        sch.do("vector", (lambda d: lambda e: e.reduce_max(C(C_LV), d[:, :], AX.X, apply_absolute_value=True))(xrt), "eV")
        vsync()
        vsmall(lambda e: e.tensor_scalar(C(C_LV), C(C_LV), 1.0, EPS_Q, ALU.mult, ALU.max))
        vsmall((lambda t_: lambda e: e.tensor_copy(C(C_SMAX + t_), C(C_LV)))(t))
        vsmall(lambda e: e.reciprocal(C(C_LR), C(C_LV)))
        vsmall(lambda e: e.tensor_scalar(C(C_LR), C(C_LR), 127.0, None, ALU.mult))
        for sl in range(4):
            sch.do("vector", (lambda d, s_: lambda e: e.tensor_scalar(
                TMP1[:, 0:512], d[:, s_ * 512:s_ * 512 + 512], C(C_LR), None, ALU.mult))(xrt, sl))
            first = True
            for j in range(7, -1, -1):
                k = float(2 ** j)
                thr = k - 128.5
                src = TMP1 if first else USCR
                sch.do("vector", (lambda s, kk, th: lambda e: e.tensor_scalar(
                    BSCR[:, 0:512], s[:, 0:512], th, -kk, ALU.is_ge, ALU.mult))(src, k, thr))
                sch.do("vector", (lambda s: lambda e: e.tensor_tensor(
                    USCR[:, 0:512], s[:, 0:512], BSCR[:, 0:512], ALU.add))(src))
                first = False
            sch.do("vector", lambda e: e.tensor_tensor(BSCR[:, 0:512], TMP1[:, 0:512], USCR[:, 0:512], ALU.subtract))
            sch.do("vector", (lambda s_: lambda e: e.tensor_scalar(
                XINTT[:, s_ * 512:s_ * 512 + 512], BSCR[:, 0:512], -128.0, None, ALU.add))(sl), "eV")
        mk_xint = sch.val["eV"]
        xint_mk[t] = mk_xint
        sch.wait("vector", "eV", mk_th)
        vsmall((lambda t_: lambda e: e.tensor_scalar(C(C_LV), C(C_SMAX + t_), I127_HI, None, ALU.mult))(t))
        vsmall((lambda t_: lambda e: e.scalar_tensor_tensor(C(C_LV), C(C_SMAX + t_), I127_LO, C(C_LV), ALU.mult, ALU.add))(t))
        vsmall((lambda t_: lambda e: e.tensor_tensor(C(C_C1 + t_), C(C_LV), C(C_M1), ALU.mult))(t))
        vsmall((lambda t_: lambda e: e.tensor_tensor(C(C_C3 + t_), C(C_LV), C(C_M2), ALU.mult))(t))
        sch.wait("sync", "eV", mk_xint)
        sch.do("sync", (lambda t_: lambda e: e.dma_start_transpose(
            XQT[:, :].rearrange("p (dc tt) -> p dc tt", dc=16)[:, :, t_ * 128:(t_ + 1) * 128],
            XINTT[:, :]))(t), "tX", 16)
        xqt_mk[t] = sch.val["tX"]

    sch.barrier()

    sch.do("vector", lambda e: e.memset(COLS[:, C_CH:C_CH + 8], 0.0), "eV")
    up_mm = {}
    tern_mk = {}
    for hb in range(NHB):
        par = hb % 2
        for mi, (mat, thp, thn, wofs) in enumerate(
                [(w1T, C_TH1P, C_TH1N, 0), (w3T, C_TH2P, C_TH2N, 8192)]):
            for pc in range(8):
                slot = (mi * 8 + pc) % 4
                st = WST[slot]
                key = ("st", slot)
                sch.wait("sync", "eV", tern_mk.get(key, 0))
                if hb >= 2:
                    sch.wait("sync", "eT", up_mm.get(hb - 2, 0))
                us_ = "u%d" % slot
                src = mat[:, hb * 256:(hb + 1) * 256].rearrange("(dc p) c -> p dc c", p=128)[:, 2 * pc:2 * pc + 2, :]
                sch.do("sync", (lambda d, s_: lambda e: e.dma_start(d[:, :].rearrange("p (dc c) -> p dc c", dc=2), s_))(st, src), us_, 16)
                mk_pc = sch.val[us_]
                sch.wait("vector", us_, mk_pc)
                dst0 = wofs + par * 4096 + pc * 512
                sch.do("vector", (lambda st_, tp: lambda e: e.tensor_scalar(
                    TMP1[:, 0:512], st_[:, :], C(tp), None, ALU.is_gt))(st, thp))
                sch.do("vector", (lambda st_, tn: lambda e: e.tensor_scalar(
                    USCR[:, 0:512], st_[:, :], C(tn), None, ALU.is_lt))(st, thn))
                sch.do("vector", (lambda d0: lambda e: e.tensor_tensor(
                    WUP[:, d0:d0 + 512], TMP1[:, 0:512], USCR[:, 0:512], ALU.subtract))(dst0), "eV")
                tern_mk[key] = sch.val["eV"]
        mk_tern = sch.val["eV"]
        for t in range(NQT):
            b1 = t % 2
            b3 = 2 + t % 2
            sch.wait("tensor", "eV", ev.get(("h1", b1), 0))
            sch.wait("tensor", "eV", ev.get(("h3", b3), 0))
            sch.wait("tensor", "eV", mk_tern)
            sch.wait("tensor", "tX", 128)
            for dc in range(NHC):
                sch.do("tensor", (lambda t_, dc_, p_, b_: lambda e: e.matmul(
                    pb[b_][:, 0:256], XQT[:, dc_ * 1024 + t_ * 128:dc_ * 1024 + t_ * 128 + 128],
                    WUP[:, p_ * 4096 + dc_ * 256:p_ * 4096 + dc_ * 256 + 256],
                    start=(dc_ == 0), stop=(dc_ == NHC - 1)))(t, dc, par, b1), None)
            for dc in range(NHC):
                sch.do("tensor", (lambda t_, dc_, p_, b_: lambda e: e.matmul(
                    pb[b_][:, 0:256], XQT[:, dc_ * 1024 + t_ * 128:dc_ * 1024 + t_ * 128 + 128],
                    WUP[:, 8192 + p_ * 4096 + dc_ * 256:8192 + p_ * 4096 + dc_ * 256 + 256],
                    start=(dc_ == 0), stop=(dc_ == NHC - 1)))(t, dc, par, b3),
                    "eT" if dc == NHC - 1 else None)
            mk_mm = sch.val["eT"]
            up_mm[hb] = mk_mm
            sch.wait("vector", "eT", mk_mm)
            sch.do("vector", (lambda b_, t_: lambda e: e.tensor_scalar(
                ST4[0][:, :], pb[b_][:, 0:256], C(C_C1 + t_), None, ALU.mult))(b1, t), "eV")
            mk_t1 = sch.val["eV"]
            ev[("h1", b1)] = mk_t1
            sch.wait("scalar", "eV", mk_t1)
            sch.do("scalar", lambda e: e.activation(ST4[1][:, :], ST4[0][:, :], AFT.Silu), "eC")
            mk_silu = sch.val["eC"]
            sch.wait("vector", "eC", mk_silu)
            ho = t * 5632 + hb * 256
            sch.do("vector", (lambda b_, t_, o: lambda e: e.scalar_tensor_tensor(
                HHt[:, o:o + 256], pb[b_][:, 0:256], C(C_C3 + t_), ST4[1][:, :], ALU.mult, ALU.mult))(b3, t, ho), "eV")
            ev[("h3", b3)] = sch.val["eV"]
            sch.do("vector", (lambda o: lambda e: e.reduce_max(
                C(C_T0), HHt[:, o:o + 256], AX.X, apply_absolute_value=True))(ho), "eV")
            vsync()
            sch.do("vector", (lambda t_: lambda e: e.tensor_tensor(C(C_CH + t_), C(C_CH + t_), C(C_T0), ALU.max))(t), "eV")

    sch.barrier()

    tr_mk = {}
    for t in range(NQT):
        vsmall((lambda t_: lambda e: e.tensor_scalar(C(C_T0), C(C_CH + t_), 1.0, EPS_Q, ALU.mult, ALU.max))(t))
        vsmall(lambda e: e.reciprocal(C(C_T1c), C(C_T0)))
        vsmall(lambda e: e.tensor_scalar(C(C_T1c), C(C_T1c), 127.0, None, ALU.mult))
        vsmall(lambda e: e.tensor_scalar(C(C_T2c), C(C_T0), I127_HI, None, ALU.mult))
        vsmall(lambda e: e.scalar_tensor_tensor(C(C_T2c), C(C_T0), I127_LO, C(C_T2c), ALU.mult, ALU.add))
        vsmall((lambda t_: lambda e: e.tensor_tensor(C(C_C2 + t_), C(C_T2c), C(C_M3), ALU.mult))(t))
        for sl in range(8):
            o0 = t * 5632 + sl * 704
            sch.do("vector", (lambda o: lambda e: e.tensor_scalar(
                TMP1[:, :], HHt[:, o:o + 704], C(C_T1c), None, ALU.mult))(o0))
            first = True
            for j in range(7, -1, -1):
                k = float(2 ** j)
                thr = k - 128.5
                src = TMP1 if first else USCR
                sch.do("vector", (lambda s, kk, th: lambda e: e.tensor_scalar(
                    BSCR[:, :], s[:, :], th, -kk, ALU.is_ge, ALU.mult))(src, k, thr))
                sch.do("vector", (lambda s: lambda e: e.tensor_tensor(USCR[:, :], s[:, :], BSCR[:, :], ALU.add))(src))
                first = False
            sch.do("vector", lambda e: e.tensor_tensor(BSCR[:, :], TMP1[:, :], USCR[:, :], ALU.subtract))
            sch.do("vector", (lambda o: lambda e: e.tensor_scalar(
                HHt[:, o:o + 704], BSCR[:, :], -128.0, None, ALU.add))(o0), "eV")
        mk_lad = sch.val["eV"]
        sch.wait("sync", "eV", mk_lad)
        if t < 4:
            dst = T1[:, :].rearrange("p (hc tt) -> p hc tt", hc=44)[:, :, t * 128:(t + 1) * 128]
        else:
            dst = T2[:, :].rearrange("p (hc tt) -> p hc tt", hc=44)[:, :, (t - 4) * 128:(t - 3) * 128]
            sch.wait("sync", "tH", tr_mk[3])
        sch.do("sync", (lambda t_, d: lambda e: e.dma_start_transpose(d, HHt[:, t_ * 5632:(t_ + 1) * 5632]))(t, dst), "tH", 16)
        tr_mk[t] = sch.val["tH"]
    mk_trdone = sch.val["tH"]

    sch.barrier()

    sch.do("vector", lambda e: e.memset(COLS[:, C_LS:C_LS + 16], 0.0), "eV")
    sch.wait("vector", "tH", mk_trdone)
    w2t_mk = {}
    zev_mk = {}
    zmm_mk = {}
    for db in range(NDB):
        par = db % 2
        if db >= 2:
            sch.wait("sync", "eT", zmm_mk[db - 2])
        for sla in range(4):
            stg = W2S[sla % 2]
            sch.wait("sync", "eV", w2t_mk.get((db - 1, sla + 2), 0) if db >= 1 else 0)
            if sla >= 2:
                sch.wait("sync", "eV", w2t_mk.get((db, sla - 2), 0))
            ss_ = "s%d" % (sla % 2)
            src = w2T[:, db * 256:(db + 1) * 256].rearrange("(hc p) c -> p hc c", p=128)[:, sla * 11:(sla + 1) * 11, :]
            sch.do("sync", (lambda d, s_: lambda e: e.dma_start(d[:, :].rearrange("p (hc c) -> p hc c", hc=11), s_))(stg, src), ss_, 16)
            mk_sl = sch.val[ss_]
            sch.wait("vector", ss_, mk_sl)
            for sl in range(4):
                o0 = sl * 704
                sch.do("vector", (lambda st_, o: lambda e: e.tensor_scalar(
                    TMP1[:, :], st_[:, o:o + 704], C(C_TH3P), None, ALU.is_gt))(stg, o0))
                sch.do("vector", (lambda st_, o: lambda e: e.tensor_scalar(
                    USCR[:, :], st_[:, o:o + 704], C(C_TH3N), None, ALU.is_lt))(stg, o0))
                sch.do("vector", (lambda p_, sa, o: lambda e: e.tensor_tensor(
                    W2P[p_][:, sa * 2816 + o:sa * 2816 + o + 704], TMP1[:, :], USCR[:, :], ALU.subtract))(par, sla, o0), "eV")
            w2t_mk[(db, sla)] = sch.val["eV"]
        mk_w2t = sch.val["eV"]
        for t in range(NQT):
            bank = par * 4 + (t % 4)
            half = (t // 4) * 256
            sch.wait("tensor", "eV", zev_mk.get((db - 2, t), 0))
            sch.wait("tensor", "eV", mk_w2t)
            if db == 0:
                sch.wait("tensor", "tH", mk_trdone)
            src_t = T1 if t < 4 else T2
            for hc in range(44):
                sch.do("tensor", (lambda s_, hc_, t_, b_, h_, p_: lambda e: e.matmul(
                    pb[b_][:, h_:h_ + 256],
                    s_[:, hc_ * 512 + (t_ % 4) * 128:hc_ * 512 + (t_ % 4) * 128 + 128],
                    W2P[p_][:, hc_ * 256:hc_ * 256 + 256],
                    start=(hc_ == 0), stop=(hc_ == 43)))(src_t, hc, t, bank, half, par),
                    "eT" if hc == 43 else None)
        mk_zmm = sch.val["eT"]
        zmm_mk[db] = mk_zmm
        for t in range(NQT):
            bank = par * 4 + (t % 4)
            half = (t // 4) * 256
            sch.wait("vector", "eT", mk_zmm)
            zsl = (lambda t_, d_: Y1N[t_][:, d_ * 256:d_ * 256 + 256])(t, db)
            sch.do("vector", (lambda b_, h_, t_, z: lambda e: e.scalar_tensor_tensor(
                z, pb[b_][:, h_:h_ + 256], C(C_C2 + t_), z, ALU.mult, ALU.add))(bank, half, t, zsl), "eV")
            zev_mk[(db, t)] = sch.val["eV"]
            mk_z = sch.val["eV"]
            sch.do("vector", (lambda z: lambda e: e.reduce_sum(C(C_T1c), z, AX.X))(zsl), "eV")
            vsync()
            sch.do("vector", (lambda t_: lambda e: e.tensor_tensor(C(C_LS + t_), C(C_LS + t_), C(C_T1c), ALU.add))(t))
            sch.do("vector", (lambda z: lambda e: e.scalar_tensor_tensor(
                TSQ[:, 0:256], z, 1.0, z, ALU.mult, ALU.mult))(zsl))
            sch.do("vector", lambda e: e.reduce_sum(C(C_T2c), TSQ[:, 0:256], AX.X), "eV")
            vsync()
            sch.do("vector", (lambda t_: lambda e: e.tensor_tensor(C(C_LQ + t_), C(C_LQ + t_), C(C_T2c), ALU.add))(t), "eV")

    sch.barrier()

    st_mk = {}
    for t in range(NQT):
        par = t % 2
        if t >= 2:
            sch.wait("vector", "a%d" % (t % 2), st_mk[t - 2])
        vsmall((lambda t_: lambda e: e.tensor_scalar(C(C_T0), C(C_LS + t_), 1.0 / 2048.0, None, ALU.mult))(t))
        vsmall((lambda t_: lambda e: e.tensor_scalar(C(C_T1c), C(C_LQ + t_), 1.0 / 2048.0, EPS_LN, ALU.mult, ALU.add))(t))
        vsmall(lambda e: e.tensor_tensor(C(C_T2c), C(C_T0), C(C_T0), ALU.mult))
        vsmall(lambda e: e.tensor_tensor(C(C_T1c), C(C_T1c), C(C_T2c), ALU.subtract))
        sch.wait("scalar", "eV", sch.val["eV"])
        sch.do("scalar", lambda e: e.activation(C(C_T2c), C(C_T1c), AFT.Sqrt), "eC")
        sch.wait("vector", "eC", sch.val["eC"])
        vsmall(lambda e: e.reciprocal(C(C_T3c), C(C_T2c)))
        for _ in range(2):
            vsmall(lambda e: e.tensor_tensor(C(C_T2c), C(C_T3c), C(C_T3c), ALU.mult))
            vsmall(lambda e: e.tensor_tensor(C(C_T2c), C(C_T2c), C(C_T1c), ALU.mult))
            vsmall(lambda e: e.tensor_scalar(C(C_T2c), C(C_T2c), -0.5, 1.5, ALU.mult, ALU.add))
            vsmall(lambda e: e.tensor_tensor(C(C_T3c), C(C_T3c), C(C_T2c), ALU.mult))
        sch.do("vector", (lambda t_, p_: lambda e: e.tensor_scalar(
            OUTT[p_][:, :], Y1N[t_][:, :], C(C_T0), C(C_T3c), ALU.subtract, ALU.mult))(t, par), "eV")
        mk_o = sch.val["eV"]
        sch.wait("scalar", "eV", mk_o)
        sch.do("scalar", (lambda t_, p_: lambda e: e.dma_start(out[t_ * 128:(t_ + 1) * 128, :], OUTT[p_][:, :]))(t, par), "a%d" % par, 16)
        st_mk[t] = sch.val["a%d" % par]
    sch.wait("scalar", "a0", sch.val["a0"])
    sch.wait("scalar", "a1", sch.val["a1"])
    sch.wait("sync", "a0", sch.val["a0"])
    sch.wait("sync", "a1", sch.val["a1"])

    if cut is not None:
        sch.ops = {e: sch.ops[e][:sch.cuts[cut][e]] for e in sch.ops}
        nv = {s: 0 for s in sch.val}
        for e in sch.ops:
            for op in sch.ops[e]:
                if op[0] == "i" and op[2] is not None:
                    nv[op[2]] += op[3]
        sch.val = nv
    import os
    dumps = os.environ.get("KDUMP", "")
    if dumps:
        sch.barrier()
        loc = locals()
        for i, nm in enumerate(dumps.split(",")):
            if ":" in nm:
                b, ix = nm.split(":")
                th = loc[b][int(ix)]
                nm = b + ix
            else:
                th = loc[nm]
            dsh = list(th.shape)
            dout = nc.dram_tensor(f"dump_{nm}", dsh, th.dtype, kind="ExternalOutput").ap()
            sch.do("sync", (lambda d, t: lambda e: e.dma_start(d, t[:, :]))(dout, th), "c0", 16)
        sch.wait("sync", "c0", sch.val["c0"])
    _QUEUE = {"c0": "sync", "m0": "sync", "m1": "sync", "x0": "sync", "x1": "sync",
              "tX": "sync", "tH": "sync", "u0": "sync", "u1": "sync", "u2": "sync",
              "u3": "sync", "s0": "sync", "s1": "sync",
              "g0": "gpsimd", "g1": "gpsimd", "gq0": "gpsimd", "gq1": "gpsimd",
              "gw1": "gpsimd", "gw2": "gpsimd", "gc": "gpsimd",
              "a0": "scalar", "a1": "scalar"}
    totals = {s: 0 for s in _QUEUE}
    for e in sch.ops:
        for op in sch.ops[e]:
            if op[0] == "i" and op[2] in totals:
                totals[op[2]] += op[3]
    for s, eng in _QUEUE.items():
        if totals[s]:
            sch.ops[eng].append(("w", s, totals[s]))
    with es:
        with nc.Block() as block:
            def make(ename):
                def body(eng):
                    for op in sch.ops[ename]:
                        if op[0] == "w":
                            eng.wait_ge(sems[op[1]], op[2])
                        else:
                            ins = op[1](eng)
                            if op[2] is not None:
                                ins.then_inc(sems[op[2]], op[3])
                return body

            block.sync(make("sync"))
            block.gpsimd(make("gpsimd"))
            block.vector(make("vector"))
            block.scalar(make("scalar"))
            block.tensor(make("tensor"))
    global _LAST_SCHED, _LAST_TENSORS
    _LAST_SCHED = sch
    _LAST_TENSORS = {k: v for k, v in locals().items()
                     if k in ("QT", "KT", "VV", "AOUT", "XQT", "XINTT", "HHt",
                              "T1", "T2", "Y1N", "TBL", "COLS", "WUP", "W2P", "XTH", "WOT")}
    return nc


_LAST_SCHED = None


_NC_CACHE = None


def _host_inputs(L0_attn_x, L0_attn_Wq, L0_attn_Wk, L0_attn_Wv, L0_attn_Wo,
                 L0_ffn_w1, L0_ffn_w2, L0_ffn_w3):
    x = np.ascontiguousarray(np.asarray(L0_attn_x, dtype=np.float32))
    wq = np.asarray(L0_attn_Wq, dtype=np.float32)
    wk = np.asarray(L0_attn_Wk, dtype=np.float32)
    wv = np.asarray(L0_attn_Wv, dtype=np.float32)
    wo = np.asarray(L0_attn_Wo, dtype=np.float32)
    w1 = np.asarray(L0_ffn_w1, dtype=np.float32)
    w2 = np.asarray(L0_ffn_w2, dtype=np.float32)
    w3 = np.asarray(L0_ffn_w3, dtype=np.float32)

    shared = {
        "wqT": np.ascontiguousarray(wq.T),
        "wkT": np.ascontiguousarray(wk.T),
        "wvT": np.ascontiguousarray(wv.T),
        "woT": np.ascontiguousarray(wo.T),
        "w1T": np.ascontiguousarray(w1.T),
        "w3T": np.ascontiguousarray(w3.T),
        "w2T": np.ascontiguousarray(w2.T),
    }
    inv32 = np.float32(1.0) / (np.float32(10000.0) **
                               (np.arange(0, HD, 2, dtype=np.float32) / np.float32(HD)))
    t32 = np.arange(S, dtype=np.float32)
    freqs = np.einsum("i,j->ij", t32, inv32)
    emb = np.concatenate([freqs, freqs], axis=1)
    sin_t = np.sin(emb).T
    cos_t = np.cos(emb).T
    p64 = np.zeros((128, 128), dtype=np.float32)
    for d in range(64):
        p64[d + 64, d] = -1.0
        p64[d, d + 64] = 1.0
    shared["p64"] = p64
    shared["ones32"] = np.ones((128, 32), dtype=np.float32)
    shared["onesr"] = np.ones((1, 128), dtype=np.float32)

    in_maps = []
    for c in range(8):
        b, half = c // 2, c % 2
        perm = np.concatenate([np.arange(half * TQ, (half + 1) * TQ),
                               np.arange((1 - half) * TQ, (2 - half) * TQ)])
        m = dict(shared)
        m["xT"] = np.ascontiguousarray(x[b].T[:, perm])
        m["xres"] = np.ascontiguousarray(x[b, half * TQ:(half + 1) * TQ, :])
        m["rope"] = np.ascontiguousarray(
            np.concatenate([sin_t[:, perm], cos_t[:, perm]], axis=1))
        in_maps.append(m)
    return in_maps


def kernel(L0_attn_x, L0_attn_Wq, L0_attn_Wk, L0_attn_Wv, L0_attn_Wo,
           L0_ffn_w1, L0_ffn_w2, L0_ffn_w3):
    global _NC_CACHE
    in_maps = _host_inputs(L0_attn_x, L0_attn_Wq, L0_attn_Wk, L0_attn_Wv,
                           L0_attn_Wo, L0_ffn_w1, L0_ffn_w2, L0_ffn_w3)
    if _NC_CACHE is None:
        _NC_CACHE = build_program()
    res = run_bass_kernel_spmd(_NC_CACHE, in_maps, core_ids=list(range(8)))
    y = np.empty((B, S, H), dtype=np.float32)
    for c in range(8):
        b, half = c // 2, c % 2
        y[b, half * TQ:(half + 1) * TQ, :] = np.asarray(res.results[c]["out"])
    return y
